# version 34
# baseline (speedup 1.0000x reference)
import contextlib
import numpy as np
import ml_dtypes
import concourse.bass as bass
import concourse.mybir as mybir
from concourse.bass_utils import run_bass_kernel_spmd

F32 = mybir.dt.float32
BF16 = mybir.dt.bfloat16
AF = mybir.ActivationFunctionType
ALU = mybir.AluOpType
AX = mybir.AxisListType

NCORES = 8
B = 4
L = 4096
C = 1024
PB = 512
HL = 516
NCH = 8
FF = 2048
NE = 8
NT = 16
EPS = 1e-5
BF16NP = ml_dtypes.bfloat16

SBUF_BASE = 16512
SBUF_TOP = 229312


class Arena:
    def __init__(self, nc, base, top, tag):
        self.nc, self.base, self.top, self.tag = nc, base, top, tag
        self.off = base
        self.n = 0

    def alloc(self, name, shape, dtype):
        nbytes = int(np.prod(shape[1:]) * np.dtype(mybir.dt.np(dtype)).itemsize)
        off = (self.off + 63) & ~63
        assert off + nbytes <= self.top, (
            f"Arena {self.tag} overflow: {name} needs {nbytes}B at {off} (top {self.top})")
        self.off = off + nbytes
        self.n += 1
        t = self.nc.alloc_sbuf_tensor_at(f"{self.tag}{self.n}_{name}", list(shape), dtype, offset=off)
        return t.ap()

    def reserve(self, nbytes):
        off = (self.off + 63) & ~63
        assert off + nbytes <= self.top
        self.off = off + nbytes
        return off

    def reset(self):
        self.off = self.base


class Sched:
    ENGS = ("tensor", "vector", "scalar", "gpsimd", "sync")

    def __init__(self, nc, stack):
        self.nc = nc
        self.ops = {e: [] for e in self.ENGS}
        self.sems = {}
        self.cnt = {}
        self.floor = {}
        self.stack = stack

    def sem(self, name):
        if name not in self.sems:
            self.sems[name] = self.stack.enter_context(self.nc.semaphore(name))
            self.cnt[name] = 0
        return self.sems[name]

    def ev(self, name):
        return (name, self.cnt.get(name, 0))

    def wait(self, eng, evs):
        best = {}
        for e in evs:
            if e is None:
                continue
            s, v = e
            if v > best.get(s, 0):
                best[s] = v
        for s, v in best.items():
            if v <= self.floor.get(s, 0):
                continue
            sem = self.sem(s)
            self.ops[eng].append(lambda E, sem=sem, v=v: E.wait_ge(sem, v))

    def run(self, eng, fn, sig=None, inc=1):
        if sig is None:
            self.ops[eng].append(fn)
            return None
        sem = self.sem(sig)
        self.cnt[sig] += inc
        self.ops[eng].append(lambda E, fn=fn, sem=sem, inc=inc: fn(E).then_inc(sem, inc))
        return (sig, self.cnt[sig])

    def barrier(self):
        snap = [(s, v) for s, v in self.cnt.items() if v > self.floor.get(s, 0)]
        for eng in self.ENGS:
            for s, v in snap:
                sem = self.sem(s)
                self.ops[eng].append(lambda E, sem=sem, v=v: E.wait_ge(sem, v))
        for s, v in snap:
            self.floor[s] = v

    def emit_all(self, block):
        @block.tensor
        def _(E):
            for fn in self.ops["tensor"]:
                fn(E)

        @block.vector
        def _(E):
            for fn in self.ops["vector"]:
                fn(E)

        @block.scalar
        def _(E):
            for fn in self.ops["scalar"]:
                fn(E)

        @block.gpsimd
        def _(E):
            for fn in self.ops["gpsimd"]:
                fn(E)

        @block.sync
        def _(E):
            for fn in self.ops["sync"]:
                fn(E)


class Rot:
    def __init__(self, slots):
        self.slots = [{"p": p, "busy": []} for p in slots]
        self.i = 0

    def get(self):
        s = self.slots[self.i % len(self.slots)]
        self.i += 1
        return s


def _bf(x):
    return np.asarray(np.asarray(x, np.float32).astype(BF16NP))


def build_program(taps=(), stop_after=None):
    taps = set(taps)
    nc = bass.Bass("TRN2", target_bir_lowering=False, detect_race_conditions=False)

    d_xh = nc.dram_tensor("xh", [B, HL, C], F32, kind="ExternalInput").ap()
    d_wqkv_a = nc.dram_tensor("wqkv_a", [C, 3 * C], F32, kind="ExternalInput").ap()
    d_wout_a = nc.dram_tensor("wout_a", [C, C], F32, kind="ExternalInput").ap()
    d_bout = nc.dram_tensor("bout", [1, C], F32, kind="ExternalInput").ap()
    d_ln = {k: nc.dram_tensor(f"ln{k}", [1, C], F32, kind="ExternalInput").ap()
            for k in ("1w", "1b", "2w", "2b")}
    d_wqkv_f = nc.dram_tensor("wqkv_f", [1536, C], F32, kind="ExternalInput").ap()
    d_wproj = nc.dram_tensor("wproj", [C, C], F32, kind="ExternalInput").ap()
    d_wrout = nc.dram_tensor("wrout", [NE, C], F32, kind="ExternalInput").ap()
    d_w1 = nc.dram_tensor("w1", [NE, C, FF], F32, kind="ExternalInput").ap()
    d_w2 = nc.dram_tensor("w2", [NE, FF, C], F32, kind="ExternalInput").ap()
    d_bcount = nc.dram_tensor("bcount", [1, PB], F32, kind="ExternalInput").ap()
    d_maskA = nc.dram_tensor("maskA", [16, PB], F32, kind="ExternalInput").ap()
    d_maskB = nc.dram_tensor("maskB", [16, PB], F32, kind="ExternalInput").ap()
    d_out = nc.dram_tensor("out", [B, PB, C], F32, kind="ExternalOutput").ap()

    d_scr_xnbf = nc.dram_tensor("scr_xnbf", [NCH, 128, B * HL], BF16, kind="Internal").ap()
    d_scr_xn = nc.dram_tensor("scr_xn", [NT, 128, C], F32, kind="Internal").ap()
    d_scr_hid = nc.dram_tensor("scr_hid", [NT, 128, C], F32, kind="Internal").ap()
    d_scr_g = nc.dram_tensor("scr_g", [NT, 128, C], F32, kind="Internal").ap()

    id_bf = nc.inline_tensor(_bf(np.eye(128)), "id_bf").ap()

    def e_mat(scale):
        E = np.zeros((NCH, 128, 16), np.float32)
        for c in range(NCH):
            for p in range(128):
                E[c, p, 2 * c + p // 64] = scale
        return E

    d_Effn = nc.inline_tensor(_bf(e_mat(0.125)), "Effn").ap()
    d_Ewin = nc.inline_tensor(_bf(e_mat(1.0 / 32.0)), "Ewin").ap()
    OHm = np.zeros((NCH, 16, 128), np.float32)
    for c in range(NCH):
        for p in range(128):
            OHm[c, 2 * c + p // 64, p] = 1.0
    d_OH = nc.inline_tensor(_bf(OHm), "OH").ap()
    d_ones1 = nc.inline_tensor(np.ones((1, 128), np.float32), "ones1").ap()

    with contextlib.ExitStack() as stack:
        sch = Sched(nc, stack)
        P = Arena(nc, SBUF_BASE, SBUF_TOP, "P")
        s_idbf = P.alloc("idbf", [128, 128], BF16)
        s_ones1 = P.alloc("ones1", [1, 128], F32)
        s_Effn = [P.alloc(f"Effn{c}", [128, 16], BF16) for c in range(NCH)]
        s_Ewin = [P.alloc(f"Ewin{c}", [128, 16], BF16) for c in range(NCH)]
        s_OH = [P.alloc(f"OH{c}", [16, 128], BF16) for c in range(NCH)]
        s_maskA = P.alloc("maskA", [16, PB], F32)
        s_maskB = P.alloc("maskB", [16, PB], F32)
        s_bcount = P.alloc("bcount", [1, PB], BF16)
        s_bout_row = P.alloc("bout_row", [1, C], BF16)
        s_W1F = P.alloc("W1F", [128, C], F32)
        s_B1F = P.alloc("B1F", [128, C], F32)
        s_W2F = P.alloc("W2F", [128, C], F32)
        s_B2F = P.alloc("B2F", [128, C], F32)
        s_wtok = [P.alloc(f"wtok{g}", [128, NE], F32) for g in range(NT)]
        s_cols = P.alloc("cols", [128, 128], F32)
        s_rt = [P.alloc(f"rt{i}", [128, 24], F32) for i in range(2)]
        s_rcols = P.alloc("rcols", [128, 32], F32)
        s_eps = P.alloc("eps", [128, 1], F32)
        s_RT = [P.alloc(f"RT{c}", [128, NE], BF16) for c in range(NCH)]

        AR = Arena(nc, P.off, SBUF_TOP, "Z")
        assert SBUF_TOP - P.off > 170000, f"arena too small: {SBUF_TOP - P.off}"

        psf = [nc.alloc_psum_tensor(f"psf{i}", [128, 512], F32).ap() for i in range(5)]
        psb = [nc.alloc_psum_tensor(f"psb{i}", [128, 1024], BF16).ap() for i in range(2)]
        pss = nc.alloc_psum_tensor("pss", [16, 512], F32).ap()
        rot_f = Rot(psf)
        rot_b = Rot(psb)

        def dma(eng, sig, out, in_, deps=()):
            sch.wait(eng, deps)
            return sch.run(eng, lambda E, o=out, i=in_: E.dma_start(out=o, in_=i), sig, 16)

        def mm(deps, out, lhsT, rhs, start, stop):
            sch.wait("tensor", deps)
            return sch.run(
                "tensor",
                lambda E, o=out, l=lhsT, r=rhs, s1=start, s2=stop: E.matmul(o, l, r, start=s1, stop=s2),
                "T", 1)

        def transp(deps, out, in_, ident):
            sch.wait("tensor", deps)
            return sch.run("tensor", lambda E, o=out, i=in_, d=ident: E.transpose(o, i, d), "T", 1)

        def act(deps, out, in_, func, bias=0.0, scale=1.0, accum_out=None):
            sch.wait("scalar", deps)
            return sch.run(
                "scalar",
                lambda E, o=out, i=in_, f=func, b=bias, s=scale, ac=accum_out:
                E.activation(o, i, f, bias=b, scale=s, accum_out=ac),
                "S", 1)

        def vec(deps, fn):
            sch.wait("vector", deps)
            return sch.run("vector", fn, "V", 1)

        def pool(deps, fn):
            sch.wait("gpsimd", deps)
            return sch.run("gpsimd", fn, "P", 1)

        def pool(deps, fn):
            sch.wait("gpsimd", deps)
            return sch.run("gpsimd", fn, "P", 1)

        def tap(name, ap, deps):
            if name not in taps:
                return
            t = nc.dram_tensor(f"tap_{name}", list(ap.shape), ap.dtype, kind="ExternalOutput").ap()
            dma("sync", "Do", t, ap, deps)


        dma("gpsimd", "Dg", s_idbf, id_bf)
        dma("gpsimd", "Dg", s_ones1, d_ones1)
        for c in range(NCH):
            dma("gpsimd", "Dg", s_Effn[c], d_Effn[c])
            dma("gpsimd", "Dg", s_Ewin[c], d_Ewin[c])
            dma("gpsimd", "Dg", s_OH[c], d_OH[c])
        dma("gpsimd", "Dg", s_maskA, d_maskA)
        dma("gpsimd", "Dg", s_maskB, d_maskB)
        dma("gpsimd", "Dg", s_bcount, d_bcount)
        dma("gpsimd", "Dg", s_bout_row, d_bout)
        AR0_BASE = AR.base
        B2_SIZE = 68 * 1024
        b2_base = AR.reserve(B2_SIZE)
        B2 = Arena(nc, b2_base, b2_base + B2_SIZE, "B2")
        b_qtraw = [B2.alloc(f"qtraw{i}", [128, C], BF16) for i in range(12)]
        b_praw = [B2.alloc(f"praw{i}", [128, C], BF16) for i in range(NCH)]
        b_wrout = B2.alloc("wrout", [NE, C], BF16)
        B2.off = B2.base
        b_l1T2 = B2.alloc("l1T", [128, NCH * 2 * C], BF16)
        b_l1T = [b_l1T2[:, c * 2 * C:(c + 1) * 2 * C] for c in range(NCH)]
        B2.off = B2.base + 44 * 1024
        b_WqkvT2 = B2.alloc("WqkvT", [128, NCH * 1536], BF16)
        b_WqkvT = [b_WqkvT2[:, c * 1536:(c + 1) * 1536] for c in range(NCH)]
        b_WprojT2 = AR.alloc("WprojT", [128, NCH * C], BF16)
        b_WprojT = [b_WprojT2[:, c * C:(c + 1) * C] for c in range(NCH)]
        AR.base = AR.off
        s_lnrow = {k: AR.alloc(f"lnrow{k}", [1, C], F32) for k in ("1w", "1b", "2w", "2b")}

        for k in ("1w", "1b", "2w", "2b"):
            dma("gpsimd", "Dg", s_lnrow[k], d_ln[k])
        ev_params = sch.ev("Dg")

        ev_eps = vec([], lambda E: E.memset(s_eps, EPS))
        ev_ms = [ev_eps,
                 vec([], lambda E: E.memset(s_cols, 0.0)),
                 vec([], lambda E: E.memset(s_rcols, 0.0)),
                 vec([], lambda E: E.memset(s_rt[0], 0.0)),
                 vec([], lambda E: E.memset(s_rt[1], 0.0))]

        ev_planes = None
        for plane, row in ((s_W1F, s_lnrow["1w"]), (s_B1F, s_lnrow["1b"]),
                           (s_W2F, s_lnrow["2w"]), (s_B2F, s_lnrow["2b"])):
            for nb in range(2):
                slot = rot_f.get()
                e1 = mm([ev_params] + slot["busy"], slot["p"], s_ones1,
                        row[:, nb * 512:(nb + 1) * 512], True, True)
                e2 = act([e1], plane[:, nb * 512:(nb + 1) * 512], slot["p"], AF.Copy)
                slot["busy"] = [e2]
                ev_planes = e2


        for i in range(12):
            dma("gpsimd", "DgQ", b_qtraw[i], d_wqkv_f[i * 128:(i + 1) * 128, :])
        ev_qtraw = [sch.ev("DgQ")] * 12
        for i in range(NCH):
            dma("gpsimd", "DgP", b_praw[i], d_wproj[i * 128:(i + 1) * 128, :])
        ev_wrout = dma("gpsimd", "DgP", b_wrout, d_wrout)
        ev_praw = [ev_wrout] * NCH

        for i in range(12):
            pb = rot_b.get()
            tr_last = None
            for c in range(NCH):
                tr_last = transp(([ev_qtraw[i]] + pb["busy"]) if c == 0 else [],
                                 pb["p"][0:128, c * 128:(c + 1) * 128],
                                 b_qtraw[i][:, c * 128:(c + 1) * 128], s_idbf)
            e_e = None
            for c in range(NCH):
                e_e = act([tr_last] if c == 0 else [], b_WqkvT[c][:, i * 128:(i + 1) * 128],
                          pb["p"][0:128, c * 128:(c + 1) * 128], AF.Copy)
            pb["busy"] = [e_e]
        for i in range(NCH):
            pb = rot_b.get()
            tr_last = None
            for c in range(NCH):
                tr_last = transp(([ev_praw[i]] + pb["busy"]) if c == 0 else [],
                                 pb["p"][0:128, c * 128:(c + 1) * 128],
                                 b_praw[i][:, c * 128:(c + 1) * 128], s_idbf)
            e_e = None
            for c in range(NCH):
                e_e = act([tr_last] if c == 0 else [], b_WprojT[c][:, i * 128:(i + 1) * 128],
                          pb["p"][0:128, c * 128:(c + 1) * 128], AF.Copy)
            pb["busy"] = [e_e]
        pb = rot_b.get()
        tr_last = None
        for c in range(NCH):
            tr_last = transp(([ev_wrout] + pb["busy"]) if c == 0 else [],
                             pb["p"][0:128, c * 128: c * 128 + NE],
                             b_wrout[0:NE, c * 128:(c + 1) * 128], s_idbf[0:NE, 0:NE])
        e_e = None
        for c in range(NCH):
            e_e = act([tr_last], s_RT[c], pb["p"][0:128, c * 128: c * 128 + NE], AF.Copy)
        pb["busy"] = [e_e]
        ev_wprep_T = sch.ev("T")

        a_x = Rot([AR.alloc(f"ax{i}", [128, C], F32) for i in range(2)])
        a_sq = AR.alloc("asq", [128, C], F32)
        a_xnb = Rot([AR.alloc(f"axnb{i}", [128, C], BF16) for i in range(2)])
        a_scr2 = AR.alloc("ascr", [128, NCH * HL], BF16)
        a_scr = [a_scr2[:, c * HL:(c + 1) * HL] for c in range(NCH)]
        a_l1 = Rot([AR.alloc(f"al1{i}", [128, C], BF16) for i in range(2)])
        a_t2 = AR.alloc("at2", [128, C], F32)

        ev_scr_bf = []
        ev_xn_wr = {}
        a_scr_wr = [None] * NCH
        state = {"ln_i": 0}

        def token_ln(xap, rows, deps_ld, sq_buf):
            cb = 16 * (state["ln_i"] % 8)
            state["ln_i"] += 1
            state["cb"] = cb
            co = lambda i: s_cols[0:rows, cb + i: cb + i + 1]
            e_sq = act(deps_ld, sq_buf[0:rows, :], xap[0:rows, :], AF.Square)
            e1 = vec(deps_ld, lambda E, x=xap, r=rows: E.reduce_sum(co(0), x[0:r, :], axis=AX.X))
            e2 = vec([e1], lambda E: E.tensor_single_scalar(co(1), co(0), -1.0 / C, ALU.mult))
            e4 = vec([e2], lambda E: E.tensor_mul(co(3), co(1), co(1)))
            e3 = vec([e_sq], lambda E, r=rows, sb=sq_buf: E.reduce_sum(co(2), sb[0:r, :], axis=AX.X))
            e6 = vec([e3], lambda E: E.scalar_tensor_tensor(co(5), co(2), 1.0 / C, co(3), ALU.mult, ALU.subtract))
            e7 = act([e6], co(6), co(5), AF.Sqrt, bias=s_eps[0:rows, :])
            e8 = vec([e7], lambda E: E.reciprocal(co(7), co(6)))
            e9 = vec([e8], lambda E, x=xap, r=rows: E.tensor_scalar(
                x[0:r, :], x[0:r, :], co(1), co(7), ALU.add, ALU.mult))
            return e9

        a_sqr = Rot([a_sq, AR.alloc("asq2", [128, C], F32)])

        def stage_alpha(b, t):
            xs = a_x.get()
            xnb = a_xnb.get()
            sq = a_sqr.get()
            rows = 128 if t < 4 else 4
            slot_sig = f"DxA{a_x.i % 2}"
            if t < 4:
                e_ld = dma("sync", slot_sig, xs["p"],
                           d_xh[b, 2 + t * 128: 2 + (t + 1) * 128, :], deps=xs["busy"])
                deps_ld = [e_ld]
            else:
                ea = dma("sync", slot_sig, xs["p"][0:2, :], d_xh[b, 0:2, :], deps=xs["busy"])
                eb = dma("sync", slot_sig, xs["p"][2:4, :], d_xh[b, 514:516, :])
                deps_ld = [ea, eb]
            e9 = token_ln(xs["p"], rows, deps_ld + ev_ms + sq["busy"], sq["p"])
            sq_reader = sch.ev("V")
            sq["busy"] = [sq_reader]
            e10 = act([e9], xnb["p"][0:rows, :], xs["p"][0:rows, :], AF.Copy)
            return dict(xs=xs, xnb=xnb, rows=rows, b=b, t=t, e9=e9, e10=e10,
                        cb=state["cb"])

        def stage_beta(cx):
            xs, xnb, rows, b, t = cx["xs"], cx["xnb"], cx["rows"], cx["b"], cx["t"]
            e9, e10 = cx["e9"], cx["e10"]
            xs_busy = [e10]
            if t < 4:
                g = b * 4 + t
                ev_xn_wr[g] = dma("scalar", f"DoG{g}", d_scr_xn[g], xs["p"], deps=[e9])
                xs_busy.append(ev_xn_wr[g])
            pb = rot_b.get()
            tr_last = None
            for c in range(NCH):
                e_t = transp(([e10] + pb["busy"]) if c == 0 else [],
                             pb["p"][0:128, c * 128: c * 128 + rows],
                             xnb["p"][0:rows, c * 128:(c + 1) * 128], s_idbf[0:rows, 0:rows])
                tr_last = e_t
            war = [w for w in a_scr_wr if w] if t == 0 else []
            e_e = None
            for c in range(NCH):
                sub = pb["p"][0:128, c * 128: c * 128 + rows]
                if t < 4:
                    e_e = vec(([tr_last] + war) if c == 0 else [],
                              lambda E, o=a_scr[c][:, 2 + t * 128: 2 + (t + 1) * 128], i=sub:
                              E.tensor_copy(o, i))
                else:
                    e_e1 = vec(([tr_last] + war) if c == 0 else [],
                               lambda E, o=a_scr[c][:, 0:2], i=pb["p"][0:128, c * 128: c * 128 + 2]:
                               E.tensor_copy(o, i))
                    e_e = vec([], lambda E, o=a_scr[c][:, 514:516],
                              i=pb["p"][0:128, c * 128 + 2: c * 128 + 4]: E.tensor_copy(o, i))
                    a_scr_wr[c] = dma("scalar", f"DoS{c}", d_scr_xnbf[c][:, b * HL:(b + 1) * HL],
                                      a_scr[c], deps=[e_e])
                    ev_scr_bf.append(a_scr_wr[c])
            pb["busy"] = [e_e]
            xnb["busy"] = [tr_last]
            if t == 4:
                xs["busy"] = xs_busy
                return
            _cb = cx["cb"]
            co2 = lambda i, cb2=_cb: s_cols[:, cb2 + 8 + i: cb2 + 8 + i + 1]
            _c5 = s_cols[:, _cb + 5: _cb + 6]
            _c6 = s_cols[:, _cb + 6: _cb + 7]
            g1 = vec([e9], lambda E, co=co2, c6=_c6: E.tensor_mul(co(0), c6, c6))
            g2 = vec([g1], lambda E, co=co2, c5=_c5: E.scalar_tensor_tensor(
                co(1), co(0), EPS, c5, ALU.mult, ALU.add))
            g3 = act([g2], co2(2), co2(1), AF.Sqrt)
            g4 = vec([g3], lambda E, co=co2: E.reciprocal(co(3), co(2)))
            g5 = vec([g4], lambda E, co=co2, c6=_c6: E.tensor_mul(co(4), c6, co(3)))
            l1 = a_l1.get()
            f10 = vec([g5, ev_planes, e9], lambda E, x=xs["p"], co=co2: E.scalar_tensor_tensor(
                a_t2, s_W1F, co(4), x, ALU.mult, ALU.mult))
            f11 = vec([f10] + l1["busy"], lambda E, o=l1["p"]: E.tensor_add(o, a_t2, s_B1F))
            xs["busy"] = xs_busy + [f10]
            pb = rot_b.get()
            tr_last = None
            for c in range(NCH):
                tr_last = transp(([f11, ev_wprep_T] + pb["busy"]) if c == 0 else [],
                                 pb["p"][0:128, c * 128:(c + 1) * 128],
                                 l1["p"][:, c * 128:(c + 1) * 128], s_idbf)
            e_e = None
            for c in range(NCH):
                e_e = act([tr_last] if c == 0 else [], b_l1T[c][:, g * 128:(g + 1) * 128],
                          pb["p"][0:128, c * 128:(c + 1) * 128], AF.Copy)
            pb["busy"] = [e_e]
            l1["busy"] = [tr_last]

        tiles = [(b, t) for b in range(B) for t in range(5)]
        prev_cx = None
        for (b, t) in tiles:
            cx = stage_alpha(b, t)
            if prev_cx is not None:
                stage_beta(prev_cx)
            prev_cx = cx
        stage_beta(prev_cx)
        ev_l1T = sch.ev("S")
        tap("l1T0", b_l1T[0], [ev_l1T])

        sch.barrier()
        AR.reset()
        tap("scrxn", d_scr_xn, [])
        tap("scrxnbf", d_scr_xnbf, [])

        if stop_after not in ("A",):
            B2.off = B2.base
            b_attn = [B2.alloc(f"attn{c}", [128, 2 * C], BF16) for c in range(NCH)]
            b_sc = [B2.alloc(f"bsc{t}", [16, 512], F32) for t in range(4)]
            b_P = [B2.alloc(f"bP{t}", [16, 512], BF16) for t in range(4)]
            b_qT = [AR.alloc(f"qT{c}", [128, 2 * C], BF16) for c in range(NCH)]
            b_kT = [AR.alloc(f"kT{j}", [128, 2 * C], BF16) for j in range(4)]
            b_vT = [AR.alloc(f"vT{j}", [128, 2 * C], BF16) for j in range(4)]
            b_xt = Rot([AR.alloc(f"bxt{i}", [128, C], F32) for i in range(2)])
            b_t1 = AR.alloc("bt1", [128, C], F32)
            b_prod = Rot([AR.alloc(f"bprod{i}", [128, 512], BF16) for i in range(2)])
            b_smax = AR.alloc("smax", [16, 512], F32)
            b_ssum = AR.alloc("ssum", [16, 512], F32)
            b_acc = AR.alloc("bacc", [128, 512], F32)

            if stop_after not in ("B1",):
                def qkv_group(col_base, M, n_out, dest_fn):
                    for oc in range(n_out):
                        cc = col_base + oc * M
                        for nb in range(4):
                            slot = rot_f.get()
                            e = None
                            for kc in range(NCH):
                                e = mm(([ev_l1T] + slot["busy"]) if kc == 0 else [],
                                       slot["p"][0:M, :],
                                       b_WqkvT[kc][:, cc:cc + M],
                                       b_l1T[kc][:, nb * 512:(nb + 1) * 512],
                                       kc == 0, kc == NCH - 1)
                            slot["busy"] = dest_fn(oc, nb, slot["p"], e)

                def q_dest(oc, nb, ps, e):
                    return [act([e], b_qT[oc][:, nb * 512:(nb + 1) * 512], ps[0:128, :], AF.Copy)]

                def kv_dest(dst):
                    def f(j, nb, ps, e):
                        e1 = act([e], dst[j][0:64, nb * 512:(nb + 1) * 512], ps[0:64, :], AF.Copy)
                        e2 = act([e1], dst[j][64:128, nb * 512:(nb + 1) * 512], ps[0:64, :], AF.Copy)
                        return [e2]
                    return f

                qkv_group(0, 128, 8, q_dest)
                qkv_group(1024, 64, 4, kv_dest(b_kT))
                qkv_group(1280, 64, 4, kv_dest(b_vT))
                ev_qkv = sch.ev("S")
                ev_qkv_T = sch.ev("T")
                tap("qT0", b_qT[0], [ev_qkv])
                tap("kT0", b_kT[0], [ev_qkv])

            if stop_after not in ("B1", "B2"):
                for c in range(NCH):
                    vec([ev_qkv], lambda E, o=b_attn[c], v=b_vT[c // 2]: E.tensor_copy(o[:, 0:512], v[:, 0:512]))
                pss_busy = []
                for s in (1, 2, 3):
                    ts = list(range(s + 1))
                    for t in ts:
                        e_mm2 = None
                        for c in range(NCH):
                            pr = b_prod.get()
                            e_p = vec([ev_qkv] + pr["busy"],
                                      lambda E, o=pr["p"], q=b_qT[c], k=b_kT[c // 2], s_=s, t_=t:
                                      E.tensor_mul(o, q[:, s_ * 512:(s_ + 1) * 512], k[:, t_ * 512:(t_ + 1) * 512]))
                            e_mm2 = mm([e_p] + (pss_busy if c == 0 else []), pss, s_Effn[c], pr["p"],
                                       c == 0, c == NCH - 1)
                            pr["busy"] = [e_mm2]
                        e_ev = act([e_mm2], b_sc[t], pss, AF.Copy)
                        if s == 1 and t == 0:
                            tap("sc10", b_sc[0], [e_ev])
                        pss_busy = [e_ev]
                    e = vec(pss_busy, lambda E, a=b_sc[0], b2=b_sc[1]: E.tensor_max(b_smax[0:16, :], a, b2))
                    for t in ts[2:]:
                        e = vec([e], lambda E, a=b_sc[t]: E.tensor_max(b_smax[0:16, :], b_smax[0:16, :], a))
                    e_exps = []
                    for t in ts:
                        ex1 = vec([e], lambda E, a=b_sc[t]: E.tensor_sub(a, a, b_smax[0:16, :]))
                        ex2 = act([ex1], b_sc[t], b_sc[t], AF.Exp)
                        e_exps.append(ex2)
                    e3 = vec(e_exps[0:2], lambda E, a=b_sc[0], b2=b_sc[1]: E.tensor_add(b_ssum[0:16, :], a, b2))
                    for t in ts[2:]:
                        e3 = vec([e3, e_exps[t]], lambda E, a=b_sc[t]: E.tensor_add(b_ssum[0:16, :], b_ssum[0:16, :], a))
                    e4 = vec([e3], lambda E: E.reciprocal(b_ssum[0:16, :], b_ssum[0:16, :]))
                    ev_p = []
                    for t in ts:
                        ev_p.append(vec([e4], lambda E, p=b_P[t], a=b_sc[t]: E.tensor_mul(p, a, b_ssum[0:16, :])))
                    if s == 1:
                        tap("P10", b_P[0], [ev_p[0]])
                        tap("P11", b_P[1], [ev_p[1]])
                    for c in range(NCH):
                        slot = rot_f.get()
                        e_acc = None
                        for t in ts:
                            e_x = mm([ev_p[t]] + slot["busy"],
                                     slot["p"], s_OH[c], b_P[t], True, True)
                            if t == 0:
                                e_acc = vec([e_x] + ([state["acc_evac"]] if state.get("acc_evac") else []),
                                            lambda E, ps=slot["p"], v=b_vT[c // 2], t_=t:
                                            E.tensor_mul(b_acc, ps, v[:, t_ * 512:(t_ + 1) * 512]))
                            else:
                                e_m2 = vec([e_x, e_acc], lambda E, ps=slot["p"], v=b_vT[c // 2], t_=t:
                                           E.tensor_mul(b_t1[:, 0:512], ps, v[:, t_ * 512:(t_ + 1) * 512]))
                                e_acc = vec([e_m2], lambda E: E.tensor_add(b_acc, b_acc, b_t1[:, 0:512]))
                            slot["busy"] = [e_acc]
                        e_cv = act([e_acc], b_attn[c][:, s * 512:(s + 1) * 512], b_acc, AF.Copy)
                        state["acc_evac"] = e_cv
                        slot["busy"] = [e_cv]
                    pss_busy = [sch.ev("S")]
                ev_attn = sch.ev("S")
                tap("attn0", b_attn[0], [ev_attn])

            if stop_after not in ("B1", "B2", "B3"):
                ev_hid = {}
                for g in range(NT):
                    xs = b_xt.get()
                    e_ld = dma("sync", f"DxB{b_xt.i % 2}", xs["p"], d_scr_xn[g], deps=xs["busy"])
                    adds = []
                    for ob in range(2):
                        slot = rot_f.get()
                        e = None
                        for kc in range(NCH):
                            e = mm(([ev_attn] + slot["busy"]) if kc == 0 else [],
                                   slot["p"],
                                   b_attn[kc][:, g * 128:(g + 1) * 128],
                                   b_WprojT[kc][:, ob * 512:(ob + 1) * 512],
                                   kc == 0, kc == NCH - 1)
                        e_a = vec([e, e_ld], lambda E, o=xs["p"], ps=slot["p"], ob_=ob:
                                  E.tensor_add(o[:, ob_ * 512:(ob_ + 1) * 512], ps,
                                               o[:, ob_ * 512:(ob_ + 1) * 512]))
                        slot["busy"] = [e_a]
                        adds.append(e_a)
                    e_wr = dma("scalar", f"DoG{g}", d_scr_hid[g], xs["p"], deps=adds)
                    ev_hid[g] = e_wr
                    xs["busy"] = [e_wr]
                tap("hidden0", None, []) if False else None

            print("ARENA B peak:", AR.off - AR.base, "of", AR.top - AR.base)
            sch.barrier()
            AR.base = AR0_BASE
            AR.reset()
            tap("scrhid", d_scr_hid, [])

        if stop_after not in ("A", "B", "B1", "B2", "B3"):
            c_xrT2 = AR.alloc("xrT", [128, NCH * 2 * C], BF16)
            c_xrT = [c_xrT2[:, c * 2 * C:(c + 1) * 2 * C] for c in range(NCH)]
            c_xrT3 = c_xrT2.rearrange("p (c l) -> p c l", l=2 * C)
            c_W1 = [AR.alloc(f"cW1{c}", [128, FF], BF16) for c in range(NCH)]
            c_W2 = [AR.alloc(f"cW2{c}", [128, C], BF16) for c in range(16)]
            c_hmid = [AR.alloc(f"chm{c}", [128, C], BF16) for c in range(16)]
            c_ht = Rot([AR.alloc(f"cht{i}", [128, C], F32) for i in range(3)])
            c_xr = Rot([AR.alloc(f"cxr{i}", [128, C], BF16) for i in range(2)])
            c_t1 = AR.alloc("ct1", [128, C], F32)
            c_t2 = AR.alloc("ct2", [128, C], F32)

            ev_xrT_tr = []
            for g in range(NT):
                ht = c_ht.get()
                cb = 8 * (g % 4)
                co = lambda i, cb=cb: s_cols[:, cb + i: cb + i + 1]
                e_ld = dma("sync", f"DxH{c_ht.i % 3}", ht["p"], d_scr_hid[g],
                           deps=[ev_hid[g]] + ht["busy"])
                e_sq = act([e_ld], c_t1, ht["p"], AF.Square)
                e1 = vec([e_ld], lambda E, x=ht["p"], co=co: E.reduce_sum(co(0), x, axis=AX.X))
                e2 = vec([e1], lambda E, co=co: E.tensor_single_scalar(co(1), co(0), -1.0 / C, ALU.mult))
                e4 = vec([e2], lambda E, co=co: E.tensor_mul(co(3), co(1), co(1)))
                e3 = vec([e_sq], lambda E, co=co: E.reduce_sum(co(2), c_t1, axis=AX.X))
                e6 = vec([e3], lambda E, co=co: E.scalar_tensor_tensor(co(5), co(2), 1.0 / C, co(3), ALU.mult, ALU.subtract))
                e7 = act([e6], co(6), co(5), AF.Sqrt, bias=s_eps)
                e8 = vec([e7], lambda E, co=co: E.reciprocal(co(7), co(6)))
                e9 = vec([e8], lambda E, x=ht["p"], co=co: E.tensor_scalar(
                    c_t2, x, co(1), co(7), ALU.add, ALU.mult))
                ht["busy"] = [e9]
                xr = c_xr.get()
                e10 = vec([e9], lambda E: E.tensor_mul(c_t2, c_t2, s_W2F))
                e11 = vec([e10] + xr["busy"], lambda E, o=xr["p"]: E.tensor_add(o, c_t2, s_B2F))
                pb = rot_b.get()
                tr_last = None
                for c in range(NCH):
                    tr_last = transp(([e11] + pb["busy"]) if c == 0 else [],
                                     pb["p"][0:128, c * 128:(c + 1) * 128],
                                     xr["p"][:, c * 128:(c + 1) * 128], s_idbf)
                e_e = None
                for c in range(NCH):
                    e_e = act([tr_last] if c == 0 else [], c_xrT[c][:, g * 128:(g + 1) * 128],
                              pb["p"][0:128, c * 128:(c + 1) * 128], AF.Copy)
                ev_tr = [e_e]
                pb["busy"] = [e_e]
                xr["busy"] = [tr_last]
                ev_xrT_tr.append(ev_tr[-1])
            for g in range(NT):
                cb = 16 * (g % 2)
                co = lambda i, cb=cb: s_rcols[:, cb + i - 8: cb + i - 8 + 1]
                slot = rot_f.get()
                e_r = None
                for kc in range(NCH):
                    e_r = mm(([ev_xrT_tr[g]] + slot["busy"]) if kc == 0 else [],
                             slot["p"][0:128, 0:NE],
                             c_xrT[kc][:, g * 128:(g + 1) * 128],
                             s_RT[kc], kc == 0, kc == NCH - 1)
                rt = s_rt[g % 2]
                rtb = state.setdefault("rtbusy", [[], []])
                e_g = act([e_r] + rtb[g % 2], rt[:, 0:NE], slot["p"][0:128, 0:NE], AF.Copy)
                slot["busy"] = [e_g]
                ee1 = vec([e_g], lambda E, rt=rt: E.reduce_max(co(8), rt[:, 0:NE], axis=AX.X, negate=True))
                ee2 = vec([ee1], lambda E, rt=rt, co=co: E.tensor_single_scalar(rt[:, 8:16], rt[:, 0:NE], co(8), ALU.add))
                ee3 = act([ee2], rt[:, 8:16], rt[:, 8:16], AF.Exp)
                ee4 = vec([ee3], lambda E, rt=rt, co=co: E.reduce_sum(co(9), rt[:, 8:16], axis=AX.X))
                ee5 = vec([ee4], lambda E, co=co: E.reciprocal(co(10), co(9)))
                ee6 = vec([ee5], lambda E, rt=rt, co=co: E.tensor_single_scalar(rt[:, 8:16], rt[:, 8:16], co(10), ALU.mult))
                ee7 = vec([ee6], lambda E, rt=rt, co=co: E.reduce_max(co(11), rt[:, 8:16], axis=AX.X))
                ee8 = vec([ee7], lambda E, rt=rt, co=co: E.tensor_single_scalar(rt[:, 0:8], rt[:, 8:16], co(11), ALU.is_ge))
                ee9 = vec([ee8], lambda E, rt=rt: E.scalar_tensor_tensor(
                    rt[:, 16:24], rt[:, 0:8], -1e30, rt[:, 8:16], ALU.mult, ALU.add))
                ee10 = vec([ee9], lambda E, rt=rt, co=co: E.reduce_max(co(12), rt[:, 16:24], axis=AX.X))
                ee11 = vec([ee10], lambda E, rt=rt, co=co: E.tensor_single_scalar(rt[:, 0:8], rt[:, 8:16], co(12), ALU.is_ge))
                ee12 = vec([ee11], lambda E, o=s_wtok[g], rt=rt: E.tensor_mul(o, rt[:, 8:16], rt[:, 0:8]))
                rtb[g % 2] = [ee12]
            ev_router = sch.ev("V")
            ev_xrT = sch.ev("S")
            tap("xrT0", c_xrT[0], [ev_xrT])
            tap("wtok0", s_wtok[0], [ev_router])

            ev_w1_free = []
            ev_w2_free = []
            for ex in range(NE):
                ev_w1_ld = None
                for kc in range(NCH):
                    ev_w1_ld = dma("gpsimd", "DgW1", c_W1[kc],
                                   d_w1[ex, kc * 128:(kc + 1) * 128, :],
                                   deps=ev_w1_free if kc == 0 else [])
                ev_w2_ld = None
                for fc in range(16):
                    ev_w2_ld = dma("gpsimd", "DgW2", c_W2[fc],
                                   d_w2[ex, fc * 128:(fc + 1) * 128, :],
                                   deps=ev_w2_free if fc == 0 else [])
                ev_w1_free = []
                ev_w2_free = []
                for th in range(2):
                    for fc in range(16):
                        for nb in range(2):
                            slot = rot_f.get()
                            e_m = None
                            for kc in range(NCH):
                                e_m = mm(([ev_w1_ld, ev_xrT] + slot["busy"]) if kc == 0 else [],
                                         slot["p"],
                                         c_W1[kc][:, fc * 128:(fc + 1) * 128],
                                         c_xrT[kc][:, th * 1024 + nb * 512: th * 1024 + (nb + 1) * 512],
                                         kc == 0, kc == NCH - 1)
                            if th == 1 and fc == 15 and nb == 1:
                                ev_w1_free = [e_m]
                            e_gl = act([e_m], c_hmid[fc][:, nb * 512:(nb + 1) * 512],
                                       slot["p"], AF.Gelu_apprx_tanh)
                            slot["busy"] = [e_gl]
                    ev_hm = sch.ev("S")
                    for gi in range(8):
                        g = th * 8 + gi
                        ht = c_ht.get()
                        e_hld = dma("sync", f"DxH{c_ht.i % 3}", ht["p"], d_scr_hid[g],
                                    deps=[ev_hid[g]] + ht["busy"])
                        accs = []
                        for ob in range(2):
                            slot = rot_f.get()
                            e_m = None
                            for fc in range(16):
                                e_m = mm(([ev_hm, ev_w2_ld] + slot["busy"]) if fc == 0 else [],
                                         slot["p"],
                                         c_hmid[fc][:, gi * 128:(gi + 1) * 128],
                                         c_W2[fc][:, ob * 512:(ob + 1) * 512],
                                         fc == 0, fc == 15)
                            if th == 1 and gi == 7 and ob == 1:
                                ev_w2_free = [e_m]
                            e_a = vec([e_m, e_hld, ev_router],
                                      lambda E, ps=slot["p"], hh=ht["p"], g_=g, ob_=ob, e_=ex:
                                      E.scalar_tensor_tensor(
                                          hh[:, ob_ * 512:(ob_ + 1) * 512], ps,
                                          s_wtok[g_][:, e_:e_ + 1],
                                          hh[:, ob_ * 512:(ob_ + 1) * 512],
                                          ALU.mult, ALU.add))
                            slot["busy"] = [e_a]
                            accs.append(e_a)
                        if ex == NE - 1:
                            e_s = act(accs, ht["p"], ht["p"], AF.Silu)
                            e_wr = dma("scalar", f"DoG{g}", d_scr_g[g], ht["p"], deps=[e_s])
                        else:
                            e_wr = dma("scalar", f"DoG{g}", d_scr_hid[g], ht["p"], deps=accs)
                        ev_hid[g] = e_wr
                        ht["busy"] = [e_wr]
            ev_g_wr = dict(ev_hid)
            tap("g0", None, []) if False else None

            print("ARENA C peak:", AR.off - AR.base, "of", AR.top - AR.base)
            sch.barrier()
            AR.reset()
            tap("scrg", d_scr_g, [])

        if stop_after not in ("A", "B", "B1", "B2", "B3", "C"):
            d_WQKV = [AR.alloc(f"WQKV{c}", [128, 3 * C], BF16) for c in range(NCH)]
            d_WOUT = [AR.alloc(f"WOUT{c}", [128, C], BF16) for c in range(NCH)]
            d_xnbf = Rot([[AR.alloc(f"dxnbf{i}_{c}", [128, HL], BF16) for c in range(NCH)]
                          for i in range(1)])
            d_qTb = [AR.alloc(f"dqT{c}", [128, HL], BF16) for c in range(NCH)]
            d_kTb = [AR.alloc(f"dkT{j}", [128, HL], BF16) for j in range(NCH)]
            d_vTb = [AR.alloc(f"dvT{j}", [128, HL], BF16) for j in range(NCH)]
            d_op = [AR.alloc(f"dop{c}", [128, PB], F32) for c in range(NCH)]
            d_opb = [AR.alloc(f"dopb{c}", [128, PB], BF16) for c in range(NCH)]
            d_scf = {dl: AR.alloc(f"dsc{dl + 3}", [16, PB], F32) for dl in range(-2, 3)}
            d_scp = {3: AR.alloc("dscp3", [16, 256], F32), -3: AR.alloc("dscm3", [16, 256], F32)}
            d_cdl = {dl: AR.alloc(f"dc{dl + 3}", [16, PB], BF16) for dl in range(-3, 4)}
            d_prod = Rot([AR.alloc(f"dprod{i}", [128, PB], BF16) for i in range(2)])
            d_gx = Rot([AR.alloc(f"dgx{i}", [128, C], F32) for i in range(2)])
            d_xb = Rot([AR.alloc(f"dxb{i}", [128, C], F32) for i in range(2)])
            d_ost = Rot([AR.alloc(f"dost{i}", [128, 512], F32) for i in range(2)])
            d_tmp2 = [AR.alloc(f"dtmp2{i}", [128, PB], F32) for i in range(2)]
            d_tmp2 = [AR.alloc(f"dtmp2{i}", [128, PB], F32) for i in range(2)]
            d_sm1 = AR.alloc("dsm1", [16, 256], F32)
            d_sm2 = AR.alloc("dsm2", [16, 256], F32)
            d_msk = AR.alloc("dmsk", [16, 256], F32)
            d_e = [AR.alloc(f"de{i}", [16, 256], F32) for i in range(4)]

            ev_dwq = None
            for c in range(NCH):
                ev_dwq = dma("gpsimd", "DgA1", d_WQKV[c], d_wqkv_a[c * 128:(c + 1) * 128, :])
            ev_dwo = None
            for c in range(NCH):
                ev_dwo = dma("gpsimd", "DgA2", d_WOUT[c], d_wout_a[c * 128:(c + 1) * 128, :])

            def d_stage2(b2, ev_op2):
                for pt in range(4):
                    g = b2 * 4 + pt
                    gx = d_gx.get()
                    xb2 = d_xb.get()
                    e_g = dma("gpsimd", f"DxG{d_gx.i % 2}", gx["p"], d_scr_g[g],
                              deps=[ev_g_wr[g]] + gx["busy"])
                    e_x = dma("gpsimd", f"DxX{d_xb.i % 2}", xb2["p"],
                              d_xh[b2, 2 + pt * 128: 2 + (pt + 1) * 128, :], deps=xb2["busy"])
                    fin = []
                    for ob in range(2):
                        ost = d_ost.get()
                        slot = rot_f.get()
                        e_m2 = None
                        for kc in range(NCH):
                            e_m2 = mm(([ev_op2, ev_dwo] + slot["busy"]) if kc == 0 else [],
                                      slot["p"],
                                      d_opb[kc][:, pt * 128:(pt + 1) * 128],
                                      d_WOUT[kc][:, ob * 512:(ob + 1) * 512],
                                      kc == 0, False)
                        e_m2 = mm([], slot["p"], s_bcount[0:1, pt * 128:(pt + 1) * 128],
                                  s_bout_row[0:1, ob * 512:(ob + 1) * 512], False, True)
                        e_1 = vec([e_m2, e_g, state.get("dtp0")],
                                  lambda E, ps=slot["p"], gg=gx["p"], ob_=ob:
                                  E.tensor_mul(d_tmp2[0], ps, gg[:, ob_ * 512:(ob_ + 1) * 512]))
                        e_2 = vec([e_1, e_x] + ost["busy"], lambda E, o=ost["p"], xx=xb2["p"], ob_=ob:
                                  E.tensor_add(o, d_tmp2[0],
                                               xx[:, ob_ * 512:(ob_ + 1) * 512]))
                        slot["busy"] = [e_2]
                        fin.append(e_2)
                        e_st = dma("scalar", f"Do{d_ost.i % 2}",
                                   d_out[b2, pt * 128:(pt + 1) * 128, ob * 512:(ob + 1) * 512],
                                   ost["p"], deps=[e_2])
                        ost["busy"] = [e_st]
                    gx["busy"] = fin
                    xb2["busy"] = fin
                state["ev_wout_T"] = sch.ev("T")

            for b in range(B):
                xnbf = d_xnbf.get()
                e_ld = None
                for c in range(NCH):
                    e_ld = dma("sync", "DxN", xnbf["p"][c],
                               d_scr_xnbf[c][:, b * HL:(b + 1) * HL],
                               deps=(xnbf["busy"] + ev_scr_bf) if c == 0 else [])
                ev_xnbf = e_ld

                def win_qkv(col_base, M, n_out, dest):
                    for oc in range(n_out):
                        cc = col_base + oc * M
                        slot = rot_f.get()
                        slot4 = rot_f.get()
                        e1 = e2 = None
                        for kc in range(NCH):
                            dep0 = ([ev_xnbf, ev_dwq] + slot["busy"] + slot4["busy"]) \
                                if kc == 0 else []
                            e1 = mm(dep0, slot["p"][0:M, :], d_WQKV[kc][:, cc:cc + M],
                                    xnbf["p"][kc][:, 0:512], kc == 0, kc == NCH - 1)
                            e2 = mm([], slot4["p"][0:M, 0:4], d_WQKV[kc][:, cc:cc + M],
                                    xnbf["p"][kc][:, 512:516], kc == 0, kc == NCH - 1)
                        evs = dest(oc, slot["p"], slot4["p"], e1, e2)
                        slot["busy"] = list(evs)
                        slot4["busy"] = list(evs)

                def full_dest_w(dst, war_key):
                    def f(oc, ps, ps4, e1, e2):
                        war = [state.get(war_key)] if oc == 0 else []
                        ea = act([e1] + war, dst[oc][:, 0:512], ps[0:128, :], AF.Copy)
                        eb = act([e2], dst[oc][:, 512:516], ps4[0:128, 0:4], AF.Copy)
                        return [ea, eb]
                    return f

                win_qkv(0, 128, 8, full_dest_w(d_qTb, "d_prods_done"))
                win_qkv(1024, 128, 8, full_dest_w(d_kTb, "d_prods_done"))
                ev_qkvb = sch.ev("S")
                if b >= 1:
                    d_stage2(b - 1, state["ev_op_prev"])
                if b == 0:
                    tap("dq0", d_qTb[0], [ev_qkvb])
                    tap("dk0", d_kTb[0], [ev_qkvb])

                sc_evs = {}
                pssb = []
                for dl in range(-2, 3):
                    e_mm2 = None
                    for c in range(NCH):
                        pr = d_prod.get()
                        e_p = vec([ev_qkvb] + pr["busy"],
                                  lambda E, o=pr["p"], q=d_qTb[c], k=d_kTb[c], d_=dl:
                                  E.tensor_mul(o, q[:, 2:514], k[:, 2 + d_:514 + d_]))
                        e_mm2 = mm([e_p] + (pssb if c == 0 else []), pss, s_Ewin[c], pr["p"],
                                   c == 0, c == NCH - 1)
                        pr["busy"] = [e_mm2]
                    e_ev = act([e_mm2], d_scf[dl], pss, AF.Copy)
                    pssb = [e_ev]
                    sc_evs[dl] = e_ev
                for dl, qoff, koff in ((3, 2, 5), (-3, 3, 0)):
                    e_mm2 = None
                    for c in range(NCH):
                        pr = d_prod.get()
                        e_p = vec([ev_qkvb] + pr["busy"],
                                  lambda E, o=pr["p"], q=d_qTb[c], k=d_kTb[c], qo=qoff, ko=koff:
                                  E.tensor_tensor(
                                      o[:, 0:256],
                                      q.rearrange("p (n two) -> p n two", two=2)[:, qo // 2: qo // 2 + 256, qo % 2],
                                      k.rearrange("p (n two) -> p n two", two=2)[:, ko // 2: ko // 2 + 256, ko % 2],
                                      ALU.mult))
                        e_mm2 = mm([e_p] + (pssb if c == 0 else []), pss[:, 0:256], s_Ewin[c],
                                   pr["p"][:, 0:256], c == 0, c == NCH - 1)
                        pr["busy"] = [e_mm2]
                    e_ev = act([e_mm2], d_scp[dl], pss[:, 0:256], AF.Copy)
                    pssb = [e_ev]
                    sc_evs[dl] = e_ev
                if b == 0:
                    tap("sc0", d_scf[0], [sc_evs[0]])

                state["d_prods_done"] = sch.ev("V")
                win_qkv(2048, 128, 8, full_dest_w(d_vTb, "d_comb_done"))
                ev_qkv_v = sch.ev("S")
                xnbf["busy"] = [sch.ev("T")]

                def sview(dl, parity):
                    if dl in (3, -3):
                        return d_scp[dl][0:16, :]
                    return d_scf[dl].rearrange("p (n two) -> p n two", two=2)[0:16, :, parity]

                def cview(dl, parity):
                    return d_cdl[dl].rearrange("p (n two) -> p n two", two=2)[0:16, :, parity]

                groups = [
                    (0, [-2, -1, 0, 1], s_maskA, False),
                    (0, [0, 1, 2, 3], s_maskB, True),
                    (1, [-3, -2, -1, 0], s_maskA, False),
                    (1, [-1, 0, 1, 2], s_maskB, True),
                ]
                e_sm_deps = list(sc_evs.values())
                ev_cd = []
                for parity, dls, mask, second in groups:
                    svs = [sview(d2, parity) for d2 in dls]
                    e = vec(e_sm_deps + ev_cd, lambda E, a=svs[0], b2=svs[1]: E.tensor_max(d_sm1, a, b2))
                    for sv in svs[2:]:
                        e = vec([e], lambda E, a=sv: E.tensor_max(d_sm1, d_sm1, a))
                    e_exps = []
                    for i2, sv in enumerate(svs):
                        ex1 = vec([e], lambda E, a=sv, o=d_e[i2]: E.tensor_sub(o, a, d_sm1))
                        ex2 = act([ex1], d_e[i2], d_e[i2], AF.Exp)
                        e_exps.append(ex2)
                    e3 = vec(e_exps[0:2], lambda E: E.tensor_add(d_sm2, d_e[0], d_e[1]))
                    for i2 in (2, 3):
                        e3 = vec([e3, e_exps[i2]], lambda E, a=d_e[i2]: E.tensor_add(d_sm2, d_sm2, a))
                    e4 = vec([e3], lambda E: E.reciprocal(d_sm2, d_sm2))
                    e5 = vec([e4], lambda E, m=mask, p2=parity: E.tensor_tensor(
                        d_msk, d_sm2, m.rearrange("p (n two) -> p n two", two=2)[0:16, :, p2], ALU.mult))
                    for idx, d2 in enumerate(dls):
                        cv = cview(d2, parity)
                        if second and idx < 2:
                            e6 = vec([e5], lambda E, a=d_e[idx]: E.tensor_mul(d_sm1, a, d_msk))
                            e7 = vec([e6], lambda E, c2=cv: E.tensor_tensor(c2, c2, d_sm1, ALU.add))
                            ev_cd.append(e7)
                        else:
                            e6 = vec([e5], lambda E, a=d_e[idx], c2=cv: E.tensor_tensor(c2, a, d_msk, ALU.mult))
                            ev_cd.append(e6)
                ev_cd_all = sch.ev("V")
                if b == 0:
                    tap("cd0", d_cdl[0], [ev_cd_all])

                for c in range(NCH):
                    e_acc = None
                    for i2, dl in enumerate([-2, -1, 0, 1, 2, 3, -3]):
                        slot = rot_f.get()
                        tp2 = d_tmp2[i2 % 2]
                        tpev = state.get(f"dtp{i2 % 2}")
                        if dl in (3, -3):
                            parity = 0 if dl == 3 else 1
                            e_x = mm([ev_cd_all] + slot["busy"], slot["p"][:, 0:256], s_OH[c],
                                     cview(dl, parity), True, True)
                            hp0 = 2 + dl + parity
                            vsh = d_vTb[c].rearrange("p (n two) -> p n two", two=2)
                            vv = vsh[:, hp0 // 2: hp0 // 2 + 256, hp0 % 2]
                            ov = d_op[c].rearrange("p (n two) -> p n two", two=2)[:, :, parity]
                            e_m2 = vec([e_x, tpev, ev_qkv_v], lambda E, ps=slot["p"], v=vv, t2=tp2: E.tensor_tensor(
                                t2[:, 0:256], ps[:, 0:256], v, ALU.mult))
                            slot["busy"] = [e_m2]
                            e_acc = vec([e_m2, e_acc], lambda E, o=ov, t2=tp2:
                                        E.tensor_tensor(o, o, t2[:, 0:256], ALU.add))
                            state[f"dtp{i2 % 2}"] = e_acc
                        else:
                            e_x = mm([ev_cd_all] + slot["busy"], slot["p"], s_OH[c], d_cdl[dl], True, True)
                            vv = d_vTb[c][:, 2 + dl: 514 + dl]
                            if i2 == 0:
                                e_acc = vec([e_x, state.get("dop_cv"), ev_qkv_v],
                                            lambda E, ps=slot["p"], v=vv, o=d_op[c]: E.tensor_mul(o, ps, v))
                                slot["busy"] = [e_acc]
                            else:
                                e_m2 = vec([e_x, tpev], lambda E, ps=slot["p"], v=vv, t2=tp2:
                                           E.tensor_mul(t2, ps, v))
                                slot["busy"] = [e_m2]
                                e_acc = vec([e_m2, e_acc], lambda E, o=d_op[c], t2=tp2:
                                            E.tensor_add(o, o, t2))
                                state[f"dtp{i2 % 2}"] = e_acc
                    e_cv = act([e_acc, state.get("ev_wout_T")], d_opb[c], d_op[c], AF.Copy)
                    state["dop_cv"] = e_cv
                ev_op = sch.ev("S")
                state["d_comb_done"] = sch.ev("V")
                state["ev_op_prev"] = ev_op
                if b == 0:
                    tap("op0", d_op[0], [ev_op])

            d_stage2(B - 1, state["ev_op_prev"])

        for _nm in ("Do0", "Do1"):
            if sch.cnt.get(_nm, 0) > 0:
                sch.ops["sync"].append(
                    lambda E, s=sch.sem(_nm), v=sch.cnt[_nm]: E.wait_ge(s, v))

        with nc.Block() as block:
            sch.emit_all(block)

    return nc


_PROG_CACHE = {}


def _get_prog():
    if "p" not in _PROG_CACHE:
        _PROG_CACHE["p"] = build_program()
    return _PROG_CACHE["p"]


def make_in_maps(inputs):
    x = np.ascontiguousarray(np.asarray(inputs["L0_attn_x"], np.float32))
    base = dict(
        wqkv_a=np.ascontiguousarray(np.asarray(inputs["L0_attn_w_qkv"], np.float32)),
        wout_a=np.ascontiguousarray(np.asarray(inputs["L0_attn_w_out"], np.float32)),
        bout=np.ascontiguousarray(np.asarray(inputs["L0_attn_b_out"], np.float32).reshape(1, C)),
        ln1w=np.ascontiguousarray(np.asarray(inputs["L0_ffn_ln1_weight"], np.float32).reshape(1, C)),
        ln1b=np.ascontiguousarray(np.asarray(inputs["L0_ffn_ln1_bias"], np.float32).reshape(1, C)),
        ln2w=np.ascontiguousarray(np.asarray(inputs["L0_ffn_ln2_weight"], np.float32).reshape(1, C)),
        ln2b=np.ascontiguousarray(np.asarray(inputs["L0_ffn_ln2_bias"], np.float32).reshape(1, C)),
        wqkv_f=np.ascontiguousarray(np.asarray(inputs["L0_ffn_qkv_weight"], np.float32)),
        wproj=np.ascontiguousarray(np.asarray(inputs["L0_ffn_proj_weight"], np.float32)),
        wrout=np.ascontiguousarray(np.asarray(inputs["L0_ffn_router_weight"], np.float32)),
        w1=np.ascontiguousarray(np.asarray(inputs["L0_ffn_moe_w1"], np.float32)),
        w2=np.ascontiguousarray(np.asarray(inputs["L0_ffn_moe_w2"], np.float32)),
    )
    in_maps = []
    for core in range(NCORES):
        lo = core * PB
        xh = np.zeros((B, HL, C), np.float32)
        s0 = max(0, lo - 2)
        e0 = min(L, lo + PB + 2)
        xh[:, (s0 - (lo - 2)):(s0 - (lo - 2)) + (e0 - s0), :] = x[:, s0:e0, :]
        bc = np.full((1, PB), 2.0, np.float32)
        mA = np.ones((16, PB), np.float32)
        mB = np.ones((16, PB), np.float32)
        if core == 0:
            bc[0, 0:2] = 1.0
            mA[:, 0:2] = 0.0
        if core == NCORES - 1:
            bc[0, PB - 2:PB] = 1.0
            mB[:, PB - 2:PB] = 0.0
        m = dict(base)
        m.update(xh=np.ascontiguousarray(xh), bcount=bc, maskA=mA, maskB=mB)
        in_maps.append(m)
    return in_maps


def kernel(**inputs):
    nc = _get_prog()
    in_maps = make_in_maps(inputs)
    res = run_bass_kernel_spmd(nc, in_maps, core_ids=list(range(NCORES)))
    out = np.empty((B, L, C), np.float32)
    for core in range(NCORES):
        out[:, core * PB:(core + 1) * PB, :] = res.results[core]["out"]
    return out


# version 35
# speedup vs baseline: 1.0064x; 1.0064x over previous
import contextlib
import numpy as np
import ml_dtypes
import concourse.bass as bass
import concourse.mybir as mybir
from concourse.bass_utils import run_bass_kernel_spmd

F32 = mybir.dt.float32
BF16 = mybir.dt.bfloat16
AF = mybir.ActivationFunctionType
ALU = mybir.AluOpType
AX = mybir.AxisListType

NCORES = 8
B = 4
L = 4096
C = 1024
PB = 512
HL = 516
NCH = 8
FF = 2048
NE = 8
NT = 16
EPS = 1e-5
BF16NP = ml_dtypes.bfloat16

SBUF_BASE = 16512
SBUF_TOP = 229312


class Arena:
    def __init__(self, nc, base, top, tag):
        self.nc, self.base, self.top, self.tag = nc, base, top, tag
        self.off = base
        self.n = 0

    def alloc(self, name, shape, dtype):
        nbytes = int(np.prod(shape[1:]) * np.dtype(mybir.dt.np(dtype)).itemsize)
        off = (self.off + 63) & ~63
        assert off + nbytes <= self.top, (
            f"Arena {self.tag} overflow: {name} needs {nbytes}B at {off} (top {self.top})")
        self.off = off + nbytes
        self.n += 1
        t = self.nc.alloc_sbuf_tensor_at(f"{self.tag}{self.n}_{name}", list(shape), dtype, offset=off)
        return t.ap()

    def reserve(self, nbytes):
        off = (self.off + 63) & ~63
        assert off + nbytes <= self.top
        self.off = off + nbytes
        return off

    def reset(self):
        self.off = self.base


class Sched:
    ENGS = ("tensor", "vector", "scalar", "gpsimd", "sync")

    def __init__(self, nc, stack):
        self.nc = nc
        self.ops = {e: [] for e in self.ENGS}
        self.sems = {}
        self.cnt = {}
        self.floor = {}
        self.stack = stack

    def sem(self, name):
        if name not in self.sems:
            self.sems[name] = self.stack.enter_context(self.nc.semaphore(name))
            self.cnt[name] = 0
        return self.sems[name]

    def ev(self, name):
        return (name, self.cnt.get(name, 0))

    def wait(self, eng, evs):
        best = {}
        for e in evs:
            if e is None:
                continue
            s, v = e
            if v > best.get(s, 0):
                best[s] = v
        for s, v in best.items():
            if v <= self.floor.get(s, 0):
                continue
            sem = self.sem(s)
            self.ops[eng].append(lambda E, sem=sem, v=v: E.wait_ge(sem, v))

    def run(self, eng, fn, sig=None, inc=1):
        if sig is None:
            self.ops[eng].append(fn)
            return None
        sem = self.sem(sig)
        self.cnt[sig] += inc
        self.ops[eng].append(lambda E, fn=fn, sem=sem, inc=inc: fn(E).then_inc(sem, inc))
        return (sig, self.cnt[sig])

    def barrier(self):
        snap = [(s, v) for s, v in self.cnt.items() if v > self.floor.get(s, 0)]
        for eng in self.ENGS:
            for s, v in snap:
                sem = self.sem(s)
                self.ops[eng].append(lambda E, sem=sem, v=v: E.wait_ge(sem, v))
        for s, v in snap:
            self.floor[s] = v

    def emit_all(self, block):
        @block.tensor
        def _(E):
            for fn in self.ops["tensor"]:
                fn(E)

        @block.vector
        def _(E):
            for fn in self.ops["vector"]:
                fn(E)

        @block.scalar
        def _(E):
            for fn in self.ops["scalar"]:
                fn(E)

        @block.gpsimd
        def _(E):
            for fn in self.ops["gpsimd"]:
                fn(E)

        @block.sync
        def _(E):
            for fn in self.ops["sync"]:
                fn(E)


class Rot:
    def __init__(self, slots):
        self.slots = [{"p": p, "busy": []} for p in slots]
        self.i = 0

    def get(self):
        s = self.slots[self.i % len(self.slots)]
        self.i += 1
        return s


def _bf(x):
    return np.asarray(np.asarray(x, np.float32).astype(BF16NP))


def build_program(taps=(), stop_after=None):
    taps = set(taps)
    nc = bass.Bass("TRN2", target_bir_lowering=False, detect_race_conditions=False)

    d_xh = nc.dram_tensor("xh", [B, HL, C], F32, kind="ExternalInput").ap()
    d_wqkv_a = nc.dram_tensor("wqkv_a", [C, 3 * C], F32, kind="ExternalInput").ap()
    d_wout_a = nc.dram_tensor("wout_a", [C, C], F32, kind="ExternalInput").ap()
    d_bout = nc.dram_tensor("bout", [1, C], F32, kind="ExternalInput").ap()
    d_ln = {k: nc.dram_tensor(f"ln{k}", [1, C], F32, kind="ExternalInput").ap()
            for k in ("1w", "1b", "2w", "2b")}
    d_wqkv_f = nc.dram_tensor("wqkv_f", [1536, C], F32, kind="ExternalInput").ap()
    d_wproj = nc.dram_tensor("wproj", [C, C], F32, kind="ExternalInput").ap()
    d_wrout = nc.dram_tensor("wrout", [NE, C], F32, kind="ExternalInput").ap()
    d_w1 = nc.dram_tensor("w1", [NE, C, FF], F32, kind="ExternalInput").ap()
    d_w2 = nc.dram_tensor("w2", [NE, FF, C], F32, kind="ExternalInput").ap()
    d_bcount = nc.dram_tensor("bcount", [1, PB], F32, kind="ExternalInput").ap()
    d_maskA = nc.dram_tensor("maskA", [16, PB], F32, kind="ExternalInput").ap()
    d_maskB = nc.dram_tensor("maskB", [16, PB], F32, kind="ExternalInput").ap()
    d_out = nc.dram_tensor("out", [B, PB, C], F32, kind="ExternalOutput").ap()

    d_scr_xnbf = nc.dram_tensor("scr_xnbf", [NCH, 128, B * HL], BF16, kind="Internal").ap()
    d_scr_xn = nc.dram_tensor("scr_xn", [NT, 128, C], F32, kind="Internal").ap()
    d_scr_hid = nc.dram_tensor("scr_hid", [NT, 128, C], F32, kind="Internal").ap()
    d_scr_g = nc.dram_tensor("scr_g", [NT, 128, C], F32, kind="Internal").ap()

    id_bf = nc.inline_tensor(_bf(np.eye(128)), "id_bf").ap()

    def e_mat(scale):
        E = np.zeros((NCH, 128, 16), np.float32)
        for c in range(NCH):
            for p in range(128):
                E[c, p, 2 * c + p // 64] = scale
        return E

    d_Effn = nc.inline_tensor(_bf(e_mat(0.125)), "Effn").ap()
    d_Ewin = nc.inline_tensor(_bf(e_mat(1.0 / 32.0)), "Ewin").ap()
    OHm = np.zeros((NCH, 16, 128), np.float32)
    for c in range(NCH):
        for p in range(128):
            OHm[c, 2 * c + p // 64, p] = 1.0
    d_OH = nc.inline_tensor(_bf(OHm), "OH").ap()
    d_ones1 = nc.inline_tensor(np.ones((1, 128), np.float32), "ones1").ap()

    with contextlib.ExitStack() as stack:
        sch = Sched(nc, stack)
        P = Arena(nc, SBUF_BASE, SBUF_TOP, "P")
        s_idbf = P.alloc("idbf", [128, 128], BF16)
        s_ones1 = P.alloc("ones1", [1, 128], F32)
        s_Effn = [P.alloc(f"Effn{c}", [128, 16], BF16) for c in range(NCH)]
        s_Ewin = [P.alloc(f"Ewin{c}", [128, 16], BF16) for c in range(NCH)]
        s_OH = [P.alloc(f"OH{c}", [16, 128], BF16) for c in range(NCH)]
        s_maskA = P.alloc("maskA", [16, PB], F32)
        s_maskB = P.alloc("maskB", [16, PB], F32)
        s_bcount = P.alloc("bcount", [1, PB], BF16)
        s_bout_row = P.alloc("bout_row", [1, C], BF16)
        s_W1F = P.alloc("W1F", [128, C], F32)
        s_B1F = P.alloc("B1F", [128, C], F32)
        s_W2F = P.alloc("W2F", [128, C], F32)
        s_B2F = P.alloc("B2F", [128, C], F32)
        s_wtok = [P.alloc(f"wtok{g}", [128, NE], F32) for g in range(NT)]
        s_cols = P.alloc("cols", [128, 64], F32)
        s_rt = [P.alloc(f"rt{i}", [128, 24], F32) for i in range(2)]
        s_rcols = P.alloc("rcols", [128, 32], F32)
        s_eps = P.alloc("eps", [128, 1], F32)
        s_RT = [P.alloc(f"RT{c}", [128, NE], BF16) for c in range(NCH)]

        AR = Arena(nc, P.off, SBUF_TOP, "Z")
        assert SBUF_TOP - P.off > 170000, f"arena too small: {SBUF_TOP - P.off}"

        psf = [nc.alloc_psum_tensor(f"psf{i}", [128, 512], F32).ap() for i in range(5)]
        psb = [nc.alloc_psum_tensor(f"psb{i}", [128, 1024], BF16).ap() for i in range(2)]
        pss = nc.alloc_psum_tensor("pss", [16, 512], F32).ap()
        rot_f = Rot(psf)
        rot_b = Rot(psb)

        def dma(eng, sig, out, in_, deps=()):
            sch.wait(eng, deps)
            return sch.run(eng, lambda E, o=out, i=in_: E.dma_start(out=o, in_=i), sig, 16)

        def mm(deps, out, lhsT, rhs, start, stop):
            sch.wait("tensor", deps)
            return sch.run(
                "tensor",
                lambda E, o=out, l=lhsT, r=rhs, s1=start, s2=stop: E.matmul(o, l, r, start=s1, stop=s2),
                "T", 1)

        def transp(deps, out, in_, ident):
            sch.wait("tensor", deps)
            return sch.run("tensor", lambda E, o=out, i=in_, d=ident: E.transpose(o, i, d), "T", 1)

        def act(deps, out, in_, func, bias=0.0, scale=1.0, accum_out=None):
            sch.wait("scalar", deps)
            return sch.run(
                "scalar",
                lambda E, o=out, i=in_, f=func, b=bias, s=scale, ac=accum_out:
                E.activation(o, i, f, bias=b, scale=s, accum_out=ac),
                "S", 1)

        def vec(deps, fn):
            sch.wait("vector", deps)
            return sch.run("vector", fn, "V", 1)

        def pool(deps, fn):
            sch.wait("gpsimd", deps)
            return sch.run("gpsimd", fn, "P", 1)

        def pool(deps, fn):
            sch.wait("gpsimd", deps)
            return sch.run("gpsimd", fn, "P", 1)

        def tap(name, ap, deps):
            if name not in taps:
                return
            t = nc.dram_tensor(f"tap_{name}", list(ap.shape), ap.dtype, kind="ExternalOutput").ap()
            dma("sync", "Do", t, ap, deps)


        dma("gpsimd", "Dg", s_idbf, id_bf)
        dma("gpsimd", "Dg", s_ones1, d_ones1)
        for c in range(NCH):
            dma("gpsimd", "Dg", s_Effn[c], d_Effn[c])
            dma("gpsimd", "Dg", s_Ewin[c], d_Ewin[c])
            dma("gpsimd", "Dg", s_OH[c], d_OH[c])
        dma("gpsimd", "Dg", s_maskA, d_maskA)
        dma("gpsimd", "Dg", s_maskB, d_maskB)
        dma("gpsimd", "Dg", s_bcount, d_bcount)
        dma("gpsimd", "Dg", s_bout_row, d_bout)
        AR0_BASE = AR.base
        B2_SIZE = 68 * 1024
        b2_base = AR.reserve(B2_SIZE)
        B2 = Arena(nc, b2_base, b2_base + B2_SIZE, "B2")
        b_qtraw = [B2.alloc(f"qtraw{i}", [128, C], BF16) for i in range(12)]
        b_praw = [B2.alloc(f"praw{i}", [128, C], BF16) for i in range(NCH)]
        b_wrout = B2.alloc("wrout", [NE, C], BF16)
        B2.off = B2.base
        b_l1T2 = B2.alloc("l1T", [128, NCH * 2 * C], BF16)
        b_l1T = [b_l1T2[:, c * 2 * C:(c + 1) * 2 * C] for c in range(NCH)]
        B2.off = B2.base + 44 * 1024
        b_WqkvT2 = B2.alloc("WqkvT", [128, NCH * 1536], BF16)
        b_WqkvT = [b_WqkvT2[:, c * 1536:(c + 1) * 1536] for c in range(NCH)]
        b_WprojT2 = AR.alloc("WprojT", [128, NCH * C], BF16)
        b_WprojT = [b_WprojT2[:, c * C:(c + 1) * C] for c in range(NCH)]
        AR.base = AR.off
        s_lnrow = {k: AR.alloc(f"lnrow{k}", [1, C], F32) for k in ("1w", "1b", "2w", "2b")}

        for k in ("1w", "1b", "2w", "2b"):
            dma("gpsimd", "Dg", s_lnrow[k], d_ln[k])
        ev_params = sch.ev("Dg")

        ev_eps = vec([], lambda E: E.memset(s_eps, EPS))
        ev_ms = [ev_eps,
                 vec([], lambda E: E.memset(s_cols, 0.0)),
                 vec([], lambda E: E.memset(s_rcols, 0.0)),
                 vec([], lambda E: E.memset(s_rt[0], 0.0)),
                 vec([], lambda E: E.memset(s_rt[1], 0.0))]

        ev_planes = None
        for plane, row in ((s_W1F, s_lnrow["1w"]), (s_B1F, s_lnrow["1b"]),
                           (s_W2F, s_lnrow["2w"]), (s_B2F, s_lnrow["2b"])):
            for nb in range(2):
                slot = rot_f.get()
                e1 = mm([ev_params] + slot["busy"], slot["p"], s_ones1,
                        row[:, nb * 512:(nb + 1) * 512], True, True)
                e2 = act([e1], plane[:, nb * 512:(nb + 1) * 512], slot["p"], AF.Copy)
                slot["busy"] = [e2]
                ev_planes = e2


        for i in range(12):
            dma("gpsimd", "DgQ", b_qtraw[i], d_wqkv_f[i * 128:(i + 1) * 128, :])
        ev_qtraw = [sch.ev("DgQ")] * 12
        for i in range(NCH):
            dma("gpsimd", "DgP", b_praw[i], d_wproj[i * 128:(i + 1) * 128, :])
        ev_wrout = dma("gpsimd", "DgP", b_wrout, d_wrout)
        ev_praw = [ev_wrout] * NCH

        for i in range(12):
            pb = rot_b.get()
            tr_last = None
            for c in range(NCH):
                tr_last = transp(([ev_qtraw[i]] + pb["busy"]) if c == 0 else [],
                                 pb["p"][0:128, c * 128:(c + 1) * 128],
                                 b_qtraw[i][:, c * 128:(c + 1) * 128], s_idbf)
            e_e = None
            for c in range(NCH):
                e_e = act([tr_last] if c == 0 else [], b_WqkvT[c][:, i * 128:(i + 1) * 128],
                          pb["p"][0:128, c * 128:(c + 1) * 128], AF.Copy)
            pb["busy"] = [e_e]
        for i in range(NCH):
            pb = rot_b.get()
            tr_last = None
            for c in range(NCH):
                tr_last = transp(([ev_praw[i]] + pb["busy"]) if c == 0 else [],
                                 pb["p"][0:128, c * 128:(c + 1) * 128],
                                 b_praw[i][:, c * 128:(c + 1) * 128], s_idbf)
            e_e = None
            for c in range(NCH):
                e_e = act([tr_last] if c == 0 else [], b_WprojT[c][:, i * 128:(i + 1) * 128],
                          pb["p"][0:128, c * 128:(c + 1) * 128], AF.Copy)
            pb["busy"] = [e_e]
        pb = rot_b.get()
        tr_last = None
        for c in range(NCH):
            tr_last = transp(([ev_wrout] + pb["busy"]) if c == 0 else [],
                             pb["p"][0:128, c * 128: c * 128 + NE],
                             b_wrout[0:NE, c * 128:(c + 1) * 128], s_idbf[0:NE, 0:NE])
        e_e = None
        for c in range(NCH):
            e_e = act([tr_last], s_RT[c], pb["p"][0:128, c * 128: c * 128 + NE], AF.Copy)
        pb["busy"] = [e_e]
        ev_wprep_T = sch.ev("T")

        a_x = Rot([AR.alloc(f"ax{i}", [128, C], F32) for i in range(2)])
        a_sq = AR.alloc("asq", [128, C], F32)
        a_xnb = Rot([AR.alloc(f"axnb{i}", [128, C], BF16) for i in range(2)])
        a_scr2 = AR.alloc("ascr", [128, NCH * HL], BF16)
        a_scr = [a_scr2[:, c * HL:(c + 1) * HL] for c in range(NCH)]
        a_l1 = Rot([AR.alloc(f"al1{i}", [128, C], BF16) for i in range(2)])
        a_t2 = AR.alloc("at2", [128, C], F32)

        ev_scr_bf = []
        ev_xn_wr = {}
        a_scr_wr = [None] * NCH
        state = {"ln_i": 0}

        def token_ln(xap, rows, deps_ld, sq_buf):
            cb = 16 * (state["ln_i"] % 4)
            state["ln_i"] += 1
            state["cb"] = cb
            co = lambda i: s_cols[0:rows, cb + i: cb + i + 1]
            e_sq = act(deps_ld, sq_buf[0:rows, :], xap[0:rows, :], AF.Square)
            e1 = vec(deps_ld, lambda E, x=xap, r=rows: E.reduce_sum(co(0), x[0:r, :], axis=AX.X))
            e2 = vec([e1], lambda E: E.tensor_single_scalar(co(1), co(0), -1.0 / C, ALU.mult))
            e4 = vec([e2], lambda E: E.tensor_mul(co(3), co(1), co(1)))
            e3 = vec([e_sq], lambda E, r=rows, sb=sq_buf: E.reduce_sum(co(2), sb[0:r, :], axis=AX.X))
            e6 = vec([e3], lambda E: E.scalar_tensor_tensor(co(5), co(2), 1.0 / C, co(3), ALU.mult, ALU.subtract))
            e7 = act([e6], co(6), co(5), AF.Sqrt, bias=s_eps[0:rows, :])
            e8 = vec([e7], lambda E: E.reciprocal(co(7), co(6)))
            e9 = vec([e8], lambda E, x=xap, r=rows: E.tensor_scalar(
                x[0:r, :], x[0:r, :], co(1), co(7), ALU.add, ALU.mult))
            return e9

        a_sqr = Rot([a_sq, AR.alloc("asq2", [128, C], F32)])

        def stage_alpha(b, t):
            xs = a_x.get()
            xnb = a_xnb.get()
            sq = a_sqr.get()
            rows = 128 if t < 4 else 4
            slot_sig = f"DxA{a_x.i % 2}"
            if t < 4:
                e_ld = dma("sync", slot_sig, xs["p"],
                           d_xh[b, 2 + t * 128: 2 + (t + 1) * 128, :], deps=xs["busy"])
                deps_ld = [e_ld]
            else:
                ea = dma("sync", slot_sig, xs["p"][0:2, :], d_xh[b, 0:2, :], deps=xs["busy"])
                eb = dma("sync", slot_sig, xs["p"][2:4, :], d_xh[b, 514:516, :])
                deps_ld = [ea, eb]
            e9 = token_ln(xs["p"], rows, deps_ld + ev_ms + sq["busy"], sq["p"])
            sq_reader = sch.ev("V")
            sq["busy"] = [sq_reader]
            e10 = act([e9], xnb["p"][0:rows, :], xs["p"][0:rows, :], AF.Copy)
            return dict(xs=xs, xnb=xnb, rows=rows, b=b, t=t, e9=e9, e10=e10,
                        cb=state["cb"])

        def stage_beta(cx):
            xs, xnb, rows, b, t = cx["xs"], cx["xnb"], cx["rows"], cx["b"], cx["t"]
            e9, e10 = cx["e9"], cx["e10"]
            xs_busy = [e10]
            if t < 4:
                g = b * 4 + t
                ev_xn_wr[g] = dma("scalar", f"DoG{g}", d_scr_xn[g], xs["p"], deps=[e9])
                xs_busy.append(ev_xn_wr[g])
            pb = rot_b.get()
            tr_last = None
            for c in range(NCH):
                e_t = transp(([e10] + pb["busy"]) if c == 0 else [],
                             pb["p"][0:128, c * 128: c * 128 + rows],
                             xnb["p"][0:rows, c * 128:(c + 1) * 128], s_idbf[0:rows, 0:rows])
                tr_last = e_t
            war = [w for w in a_scr_wr if w] if t == 0 else []
            e_e = None
            for c in range(NCH):
                sub = pb["p"][0:128, c * 128: c * 128 + rows]
                if t < 4:
                    e_e = vec(([tr_last] + war) if c == 0 else [],
                              lambda E, o=a_scr[c][:, 2 + t * 128: 2 + (t + 1) * 128], i=sub:
                              E.tensor_copy(o, i))
                else:
                    e_e1 = vec(([tr_last] + war) if c == 0 else [],
                               lambda E, o=a_scr[c][:, 0:2], i=pb["p"][0:128, c * 128: c * 128 + 2]:
                               E.tensor_copy(o, i))
                    e_e = vec([], lambda E, o=a_scr[c][:, 514:516],
                              i=pb["p"][0:128, c * 128 + 2: c * 128 + 4]: E.tensor_copy(o, i))
                    a_scr_wr[c] = dma("scalar", f"DoS{c}", d_scr_xnbf[c][:, b * HL:(b + 1) * HL],
                                      a_scr[c], deps=[e_e])
                    ev_scr_bf.append(a_scr_wr[c])
            pb["busy"] = [e_e]
            xnb["busy"] = [tr_last]
            if t == 4:
                xs["busy"] = xs_busy
                return
            _cb = cx["cb"]
            co2 = lambda i, cb2=_cb: s_cols[:, cb2 + 8 + i: cb2 + 8 + i + 1]
            _c5 = s_cols[:, _cb + 5: _cb + 6]
            _c6 = s_cols[:, _cb + 6: _cb + 7]
            g1 = vec([e9], lambda E, co=co2, c6=_c6: E.tensor_mul(co(0), c6, c6))
            g2 = vec([g1], lambda E, co=co2, c5=_c5: E.scalar_tensor_tensor(
                co(1), co(0), EPS, c5, ALU.mult, ALU.add))
            g3 = act([g2], co2(2), co2(1), AF.Sqrt)
            g4 = vec([g3], lambda E, co=co2: E.reciprocal(co(3), co(2)))
            g5 = vec([g4], lambda E, co=co2, c6=_c6: E.tensor_mul(co(4), c6, co(3)))
            l1 = a_l1.get()
            f10 = vec([g5, ev_planes, e9], lambda E, x=xs["p"], co=co2: E.scalar_tensor_tensor(
                a_t2, s_W1F, co(4), x, ALU.mult, ALU.mult))
            f11 = vec([f10] + l1["busy"], lambda E, o=l1["p"]: E.tensor_add(o, a_t2, s_B1F))
            xs["busy"] = xs_busy + [f10]
            pb = rot_b.get()
            tr_last = None
            for c in range(NCH):
                tr_last = transp(([f11, ev_wprep_T] + pb["busy"]) if c == 0 else [],
                                 pb["p"][0:128, c * 128:(c + 1) * 128],
                                 l1["p"][:, c * 128:(c + 1) * 128], s_idbf)
            e_e = None
            for c in range(NCH):
                e_e = act([tr_last] if c == 0 else [], b_l1T[c][:, g * 128:(g + 1) * 128],
                          pb["p"][0:128, c * 128:(c + 1) * 128], AF.Copy)
            pb["busy"] = [e_e]
            l1["busy"] = [tr_last]

        tiles = [(b, t) for b in range(B) for t in range(5)]
        prev_cx = None
        for (b, t) in tiles:
            cx = stage_alpha(b, t)
            if prev_cx is not None:
                stage_beta(prev_cx)
            prev_cx = cx
        stage_beta(prev_cx)
        ev_l1T = sch.ev("S")
        tap("l1T0", b_l1T[0], [ev_l1T])

        sch.barrier()
        AR.reset()
        tap("scrxn", d_scr_xn, [])
        tap("scrxnbf", d_scr_xnbf, [])

        if stop_after not in ("A",):
            B2.off = B2.base
            b_attn = [B2.alloc(f"attn{c}", [128, 2 * C], BF16) for c in range(NCH)]
            b_sc = [B2.alloc(f"bsc{t}", [16, 512], F32) for t in range(4)]
            b_P = [B2.alloc(f"bP{t}", [16, 512], BF16) for t in range(4)]
            b_qT = [AR.alloc(f"qT{c}", [128, 2 * C], BF16) for c in range(NCH)]
            b_kT = [AR.alloc(f"kT{j}", [128, 2 * C], BF16) for j in range(4)]
            b_vT = [AR.alloc(f"vT{j}", [128, 2 * C], BF16) for j in range(4)]
            b_xt = Rot([AR.alloc(f"bxt{i}", [128, C], F32) for i in range(2)])
            b_t1 = AR.alloc("bt1", [128, C], F32)
            b_prod = Rot([AR.alloc(f"bprod{i}", [128, 512], BF16) for i in range(2)])
            b_smax = AR.alloc("smax", [16, 512], F32)
            b_ssum = AR.alloc("ssum", [16, 512], F32)
            b_acc = AR.alloc("bacc", [128, 512], F32)

            if stop_after not in ("B1",):
                def qkv_group(col_base, M, n_out, dest_fn):
                    for oc in range(n_out):
                        cc = col_base + oc * M
                        for nb in range(4):
                            slot = rot_f.get()
                            e = None
                            for kc in range(NCH):
                                e = mm(([ev_l1T] + slot["busy"]) if kc == 0 else [],
                                       slot["p"][0:M, :],
                                       b_WqkvT[kc][:, cc:cc + M],
                                       b_l1T[kc][:, nb * 512:(nb + 1) * 512],
                                       kc == 0, kc == NCH - 1)
                            slot["busy"] = dest_fn(oc, nb, slot["p"], e)

                def q_dest(oc, nb, ps, e):
                    return [act([e], b_qT[oc][:, nb * 512:(nb + 1) * 512], ps[0:128, :], AF.Copy)]

                def kv_dest(dst):
                    def f(j, nb, ps, e):
                        e1 = act([e], dst[j][0:64, nb * 512:(nb + 1) * 512], ps[0:64, :], AF.Copy)
                        e2 = act([e1], dst[j][64:128, nb * 512:(nb + 1) * 512], ps[0:64, :], AF.Copy)
                        return [e2]
                    return f

                qkv_group(0, 128, 8, q_dest)
                qkv_group(1024, 64, 4, kv_dest(b_kT))
                qkv_group(1280, 64, 4, kv_dest(b_vT))
                ev_qkv = sch.ev("S")
                ev_qkv_T = sch.ev("T")
                tap("qT0", b_qT[0], [ev_qkv])
                tap("kT0", b_kT[0], [ev_qkv])

            if stop_after not in ("B1", "B2"):
                for c in range(NCH):
                    vec([ev_qkv], lambda E, o=b_attn[c], v=b_vT[c // 2]: E.tensor_copy(o[:, 0:512], v[:, 0:512]))
                pss_busy = []
                for s in (1, 2, 3):
                    ts = list(range(s + 1))
                    for t in ts:
                        e_mm2 = None
                        for c in range(NCH):
                            pr = b_prod.get()
                            e_p = vec([ev_qkv] + pr["busy"],
                                      lambda E, o=pr["p"], q=b_qT[c], k=b_kT[c // 2], s_=s, t_=t:
                                      E.tensor_mul(o, q[:, s_ * 512:(s_ + 1) * 512], k[:, t_ * 512:(t_ + 1) * 512]))
                            e_mm2 = mm([e_p] + (pss_busy if c == 0 else []), pss, s_Effn[c], pr["p"],
                                       c == 0, c == NCH - 1)
                            pr["busy"] = [e_mm2]
                        e_ev = act([e_mm2], b_sc[t], pss, AF.Copy)
                        if s == 1 and t == 0:
                            tap("sc10", b_sc[0], [e_ev])
                        pss_busy = [e_ev]
                    e = vec(pss_busy, lambda E, a=b_sc[0], b2=b_sc[1]: E.tensor_max(b_smax[0:16, :], a, b2))
                    for t in ts[2:]:
                        e = vec([e], lambda E, a=b_sc[t]: E.tensor_max(b_smax[0:16, :], b_smax[0:16, :], a))
                    e_exps = []
                    for t in ts:
                        ex1 = vec([e], lambda E, a=b_sc[t]: E.tensor_sub(a, a, b_smax[0:16, :]))
                        ex2 = act([ex1], b_sc[t], b_sc[t], AF.Exp)
                        e_exps.append(ex2)
                    e3 = vec(e_exps[0:2], lambda E, a=b_sc[0], b2=b_sc[1]: E.tensor_add(b_ssum[0:16, :], a, b2))
                    for t in ts[2:]:
                        e3 = vec([e3, e_exps[t]], lambda E, a=b_sc[t]: E.tensor_add(b_ssum[0:16, :], b_ssum[0:16, :], a))
                    e4 = vec([e3], lambda E: E.reciprocal(b_ssum[0:16, :], b_ssum[0:16, :]))
                    ev_p = []
                    for t in ts:
                        ev_p.append(vec([e4], lambda E, p=b_P[t], a=b_sc[t]: E.tensor_mul(p, a, b_ssum[0:16, :])))
                    if s == 1:
                        tap("P10", b_P[0], [ev_p[0]])
                        tap("P11", b_P[1], [ev_p[1]])
                    for c in range(NCH):
                        slot = rot_f.get()
                        e_acc = None
                        for t in ts:
                            e_x = mm([ev_p[t]] + slot["busy"],
                                     slot["p"], s_OH[c], b_P[t], True, True)
                            if t == 0:
                                e_acc = vec([e_x] + ([state["acc_evac"]] if state.get("acc_evac") else []),
                                            lambda E, ps=slot["p"], v=b_vT[c // 2], t_=t:
                                            E.tensor_mul(b_acc, ps, v[:, t_ * 512:(t_ + 1) * 512]))
                            else:
                                e_m2 = vec([e_x, e_acc], lambda E, ps=slot["p"], v=b_vT[c // 2], t_=t:
                                           E.tensor_mul(b_t1[:, 0:512], ps, v[:, t_ * 512:(t_ + 1) * 512]))
                                e_acc = vec([e_m2], lambda E: E.tensor_add(b_acc, b_acc, b_t1[:, 0:512]))
                            slot["busy"] = [e_acc]
                        e_cv = act([e_acc], b_attn[c][:, s * 512:(s + 1) * 512], b_acc, AF.Copy)
                        state["acc_evac"] = e_cv
                        slot["busy"] = [e_cv]
                    pss_busy = [sch.ev("S")]
                ev_attn = sch.ev("S")
                tap("attn0", b_attn[0], [ev_attn])

            if stop_after not in ("B1", "B2", "B3"):
                ev_hid = {}
                for g in range(NT):
                    xs = b_xt.get()
                    e_ld = dma("sync", f"DxB{b_xt.i % 2}", xs["p"], d_scr_xn[g], deps=xs["busy"])
                    adds = []
                    for ob in range(2):
                        slot = rot_f.get()
                        e = None
                        for kc in range(NCH):
                            e = mm(([ev_attn] + slot["busy"]) if kc == 0 else [],
                                   slot["p"],
                                   b_attn[kc][:, g * 128:(g + 1) * 128],
                                   b_WprojT[kc][:, ob * 512:(ob + 1) * 512],
                                   kc == 0, kc == NCH - 1)
                        e_a = vec([e, e_ld], lambda E, o=xs["p"], ps=slot["p"], ob_=ob:
                                  E.tensor_add(o[:, ob_ * 512:(ob_ + 1) * 512], ps,
                                               o[:, ob_ * 512:(ob_ + 1) * 512]))
                        slot["busy"] = [e_a]
                        adds.append(e_a)
                    e_wr = dma("scalar", f"DoG{g}", d_scr_hid[g], xs["p"], deps=adds)
                    ev_hid[g] = e_wr
                    xs["busy"] = [e_wr]
                tap("hidden0", None, []) if False else None

            print("ARENA B peak:", AR.off - AR.base, "of", AR.top - AR.base)
            sch.barrier()
            AR.base = AR0_BASE
            AR.reset()
            tap("scrhid", d_scr_hid, [])

        if stop_after not in ("A", "B", "B1", "B2", "B3"):
            c_xrT2 = AR.alloc("xrT", [128, NCH * 2 * C], BF16)
            c_xrT = [c_xrT2[:, c * 2 * C:(c + 1) * 2 * C] for c in range(NCH)]
            c_xrT3 = c_xrT2.rearrange("p (c l) -> p c l", l=2 * C)
            c_W1 = [AR.alloc(f"cW1{c}", [128, FF], BF16) for c in range(NCH)]
            c_W2 = [AR.alloc(f"cW2{c}", [128, C], BF16) for c in range(16)]
            c_hmid = [AR.alloc(f"chm{c}", [128, C], BF16) for c in range(16)]
            c_ht = Rot([AR.alloc(f"cht{i}", [128, C], F32) for i in range(3)])
            c_xr = Rot([AR.alloc(f"cxr{i}", [128, C], BF16) for i in range(2)])
            c_t1 = AR.alloc("ct1", [128, C], F32)
            c_t2 = AR.alloc("ct2", [128, C], F32)

            ev_xrT_tr = []
            for g in range(NT):
                ht = c_ht.get()
                cb = 8 * (g % 4)
                co = lambda i, cb=cb: s_cols[:, cb + i: cb + i + 1]
                e_ld = dma("sync", f"DxH{c_ht.i % 3}", ht["p"], d_scr_hid[g],
                           deps=[ev_hid[g]] + ht["busy"])
                e_sq = act([e_ld], c_t1, ht["p"], AF.Square)
                e1 = vec([e_ld], lambda E, x=ht["p"], co=co: E.reduce_sum(co(0), x, axis=AX.X))
                e2 = vec([e1], lambda E, co=co: E.tensor_single_scalar(co(1), co(0), -1.0 / C, ALU.mult))
                e4 = vec([e2], lambda E, co=co: E.tensor_mul(co(3), co(1), co(1)))
                e3 = vec([e_sq], lambda E, co=co: E.reduce_sum(co(2), c_t1, axis=AX.X))
                e6 = vec([e3], lambda E, co=co: E.scalar_tensor_tensor(co(5), co(2), 1.0 / C, co(3), ALU.mult, ALU.subtract))
                e7 = act([e6], co(6), co(5), AF.Sqrt, bias=s_eps)
                e8 = vec([e7], lambda E, co=co: E.reciprocal(co(7), co(6)))
                e9 = vec([e8], lambda E, x=ht["p"], co=co: E.tensor_scalar(
                    c_t2, x, co(1), co(7), ALU.add, ALU.mult))
                ht["busy"] = [e9]
                xr = c_xr.get()
                e10 = vec([e9], lambda E: E.tensor_mul(c_t2, c_t2, s_W2F))
                e11 = vec([e10] + xr["busy"], lambda E, o=xr["p"]: E.tensor_add(o, c_t2, s_B2F))
                pb = rot_b.get()
                tr_last = None
                for c in range(NCH):
                    tr_last = transp(([e11] + pb["busy"]) if c == 0 else [],
                                     pb["p"][0:128, c * 128:(c + 1) * 128],
                                     xr["p"][:, c * 128:(c + 1) * 128], s_idbf)
                e_e = None
                for c in range(NCH):
                    e_e = act([tr_last] if c == 0 else [], c_xrT[c][:, g * 128:(g + 1) * 128],
                              pb["p"][0:128, c * 128:(c + 1) * 128], AF.Copy)
                ev_tr = [e_e]
                pb["busy"] = [e_e]
                xr["busy"] = [tr_last]
                ev_xrT_tr.append(ev_tr[-1])
            for g in range(NT):
                cb = 16 * (g % 2)
                co = lambda i, cb=cb: s_rcols[:, cb + i - 8: cb + i - 8 + 1]
                slot = rot_f.get()
                e_r = None
                for kc in range(NCH):
                    e_r = mm(([ev_xrT_tr[g]] + slot["busy"]) if kc == 0 else [],
                             slot["p"][0:128, 0:NE],
                             c_xrT[kc][:, g * 128:(g + 1) * 128],
                             s_RT[kc], kc == 0, kc == NCH - 1)
                rt = s_rt[g % 2]
                rtb = state.setdefault("rtbusy", [[], []])
                e_g = act([e_r] + rtb[g % 2], rt[:, 0:NE], slot["p"][0:128, 0:NE], AF.Copy)
                slot["busy"] = [e_g]
                ee1 = vec([e_g], lambda E, rt=rt: E.reduce_max(co(8), rt[:, 0:NE], axis=AX.X, negate=True))
                ee2 = vec([ee1], lambda E, rt=rt, co=co: E.tensor_single_scalar(rt[:, 8:16], rt[:, 0:NE], co(8), ALU.add))
                ee3 = act([ee2], rt[:, 8:16], rt[:, 8:16], AF.Exp)
                ee4 = vec([ee3], lambda E, rt=rt, co=co: E.reduce_sum(co(9), rt[:, 8:16], axis=AX.X))
                ee5 = vec([ee4], lambda E, co=co: E.reciprocal(co(10), co(9)))
                ee6 = vec([ee5], lambda E, rt=rt, co=co: E.tensor_single_scalar(rt[:, 8:16], rt[:, 8:16], co(10), ALU.mult))
                ee7 = vec([ee6], lambda E, rt=rt, co=co: E.reduce_max(co(11), rt[:, 8:16], axis=AX.X))
                ee8 = vec([ee7], lambda E, rt=rt, co=co: E.tensor_single_scalar(rt[:, 0:8], rt[:, 8:16], co(11), ALU.is_ge))
                ee9 = vec([ee8], lambda E, rt=rt: E.scalar_tensor_tensor(
                    rt[:, 16:24], rt[:, 0:8], -1e30, rt[:, 8:16], ALU.mult, ALU.add))
                ee10 = vec([ee9], lambda E, rt=rt, co=co: E.reduce_max(co(12), rt[:, 16:24], axis=AX.X))
                ee11 = vec([ee10], lambda E, rt=rt, co=co: E.tensor_single_scalar(rt[:, 0:8], rt[:, 8:16], co(12), ALU.is_ge))
                ee12 = vec([ee11], lambda E, o=s_wtok[g], rt=rt: E.tensor_mul(o, rt[:, 8:16], rt[:, 0:8]))
                rtb[g % 2] = [ee12]
            ev_router = sch.ev("V")
            ev_xrT = sch.ev("S")
            tap("xrT0", c_xrT[0], [ev_xrT])
            tap("wtok0", s_wtok[0], [ev_router])

            ev_w1_free = []
            ev_w2_free = []
            for ex in range(NE):
                ev_w1_ld = None
                for kc in range(NCH):
                    ev_w1_ld = dma("gpsimd", "DgW1", c_W1[kc],
                                   d_w1[ex, kc * 128:(kc + 1) * 128, :],
                                   deps=ev_w1_free if kc == 0 else [])
                ev_w2_ld = None
                for fc in range(16):
                    ev_w2_ld = dma("gpsimd", "DgW2", c_W2[fc],
                                   d_w2[ex, fc * 128:(fc + 1) * 128, :],
                                   deps=ev_w2_free if fc == 0 else [])
                ev_w1_free = []
                ev_w2_free = []
                for th in range(2):
                    for fc in range(16):
                        for nb in range(2):
                            slot = rot_f.get()
                            e_m = None
                            for kc in range(NCH):
                                e_m = mm(([ev_w1_ld, ev_xrT] + slot["busy"]) if kc == 0 else [],
                                         slot["p"],
                                         c_W1[kc][:, fc * 128:(fc + 1) * 128],
                                         c_xrT[kc][:, th * 1024 + nb * 512: th * 1024 + (nb + 1) * 512],
                                         kc == 0, kc == NCH - 1)
                            if th == 1 and fc == 15 and nb == 1:
                                ev_w1_free = [e_m]
                            e_gl = act([e_m], c_hmid[fc][:, nb * 512:(nb + 1) * 512],
                                       slot["p"], AF.Gelu_apprx_tanh)
                            slot["busy"] = [e_gl]
                    ev_hm = sch.ev("S")
                    for gi in range(8):
                        g = th * 8 + gi
                        ht = c_ht.get()
                        e_hld = dma("sync", f"DxH{c_ht.i % 3}", ht["p"], d_scr_hid[g],
                                    deps=[ev_hid[g]] + ht["busy"])
                        accs = []
                        for ob in range(2):
                            slot = rot_f.get()
                            e_m = None
                            for fc in range(16):
                                e_m = mm(([ev_hm, ev_w2_ld] + slot["busy"]) if fc == 0 else [],
                                         slot["p"],
                                         c_hmid[fc][:, gi * 128:(gi + 1) * 128],
                                         c_W2[fc][:, ob * 512:(ob + 1) * 512],
                                         fc == 0, fc == 15)
                            if th == 1 and gi == 7 and ob == 1:
                                ev_w2_free = [e_m]
                            e_a = vec([e_m, e_hld, ev_router],
                                      lambda E, ps=slot["p"], hh=ht["p"], g_=g, ob_=ob, e_=ex:
                                      E.scalar_tensor_tensor(
                                          hh[:, ob_ * 512:(ob_ + 1) * 512], ps,
                                          s_wtok[g_][:, e_:e_ + 1],
                                          hh[:, ob_ * 512:(ob_ + 1) * 512],
                                          ALU.mult, ALU.add))
                            slot["busy"] = [e_a]
                            accs.append(e_a)
                        if ex == NE - 1:
                            e_s = act(accs, ht["p"], ht["p"], AF.Silu)
                            e_wr = dma("scalar", f"DoG{g}", d_scr_g[g], ht["p"], deps=[e_s])
                        else:
                            e_wr = dma("scalar", f"DoG{g}", d_scr_hid[g], ht["p"], deps=accs)
                        ev_hid[g] = e_wr
                        ht["busy"] = [e_wr]
            ev_g_wr = dict(ev_hid)
            tap("g0", None, []) if False else None

            print("ARENA C peak:", AR.off - AR.base, "of", AR.top - AR.base)
            sch.barrier()
            AR.reset()
            tap("scrg", d_scr_g, [])

        if stop_after not in ("A", "B", "B1", "B2", "B3", "C"):
            d_WQKV = [AR.alloc(f"WQKV{c}", [128, 3 * C], BF16) for c in range(NCH)]
            d_WOUT = [AR.alloc(f"WOUT{c}", [128, C], BF16) for c in range(NCH)]
            d_xnbf2 = AR.alloc("dxnbf2", [128, NCH * HL], BF16)
            d_xnbf = Rot([[d_xnbf2[:, c * HL:(c + 1) * HL] for c in range(NCH)]])
            d_xnbf3 = d_xnbf2.rearrange("p (c l) -> p c l", l=HL)
            d_scr_xnbf_p = d_scr_xnbf.rearrange("c p l -> p c l")
            d_qTb = [AR.alloc(f"dqT{c}", [128, HL], BF16) for c in range(NCH)]
            d_kTb = [AR.alloc(f"dkT{j}", [128, HL], BF16) for j in range(NCH)]
            d_vTb = [AR.alloc(f"dvT{j}", [128, HL], BF16) for j in range(NCH)]
            d_op = [AR.alloc(f"dop{c}", [128, PB], F32) for c in range(NCH)]
            d_opb = [AR.alloc(f"dopb{c}", [128, PB], BF16) for c in range(NCH)]
            d_scf = {dl: AR.alloc(f"dsc{dl + 3}", [16, PB], F32) for dl in range(-2, 3)}
            d_scp = {3: AR.alloc("dscp3", [16, 256], F32), -3: AR.alloc("dscm3", [16, 256], F32)}
            d_cdl = {dl: AR.alloc(f"dc{dl + 3}", [16, PB], BF16) for dl in range(-3, 4)}
            d_prod = Rot([AR.alloc(f"dprod{i}", [128, PB], BF16) for i in range(2)])
            d_gx = Rot([AR.alloc(f"dgx{i}", [128, C], F32) for i in range(2)])
            d_xb = Rot([AR.alloc(f"dxb{i}", [128, C], F32) for i in range(2)])
            d_ost = Rot([AR.alloc(f"dost{i}", [128, 512], F32) for i in range(2)])
            d_tmp2 = [AR.alloc(f"dtmp2{i}", [128, PB], F32) for i in range(2)]
            d_tmp2 = [AR.alloc(f"dtmp2{i}", [128, PB], F32) for i in range(2)]
            d_sm1 = AR.alloc("dsm1", [16, 256], F32)
            d_sm2 = AR.alloc("dsm2", [16, 256], F32)
            d_msk = AR.alloc("dmsk", [16, 256], F32)
            d_e = [AR.alloc(f"de{i}", [16, 256], F32) for i in range(4)]

            ev_dwq = None
            for c in range(NCH):
                ev_dwq = dma("gpsimd", "DgA1", d_WQKV[c], d_wqkv_a[c * 128:(c + 1) * 128, :])
            ev_dwo = None
            for c in range(NCH):
                ev_dwo = dma("gpsimd", "DgA2", d_WOUT[c], d_wout_a[c * 128:(c + 1) * 128, :])

            def d_stage2(b2, ev_op2):
                for pt in range(4):
                    g = b2 * 4 + pt
                    gx = d_gx.get()
                    xb2 = d_xb.get()
                    e_g = dma("sync", f"DxG{d_gx.i % 2}", gx["p"], d_scr_g[g],
                              deps=[ev_g_wr[g]] + gx["busy"])
                    e_x = dma("sync", f"DxX{d_xb.i % 2}", xb2["p"],
                              d_xh[b2, 2 + pt * 128: 2 + (pt + 1) * 128, :], deps=xb2["busy"])
                    fin = []
                    for ob in range(2):
                        ost = d_ost.get()
                        slot = rot_f.get()
                        e_m2 = None
                        for kc in range(NCH):
                            e_m2 = mm(([ev_op2, ev_dwo] + slot["busy"]) if kc == 0 else [],
                                      slot["p"],
                                      d_opb[kc][:, pt * 128:(pt + 1) * 128],
                                      d_WOUT[kc][:, ob * 512:(ob + 1) * 512],
                                      kc == 0, False)
                        e_m2 = mm([], slot["p"], s_bcount[0:1, pt * 128:(pt + 1) * 128],
                                  s_bout_row[0:1, ob * 512:(ob + 1) * 512], False, True)
                        e_1 = vec([e_m2, e_g, state.get("dtp0")],
                                  lambda E, ps=slot["p"], gg=gx["p"], ob_=ob:
                                  E.tensor_mul(d_tmp2[0], ps, gg[:, ob_ * 512:(ob_ + 1) * 512]))
                        e_2 = vec([e_1, e_x] + ost["busy"], lambda E, o=ost["p"], xx=xb2["p"], ob_=ob:
                                  E.tensor_add(o, d_tmp2[0],
                                               xx[:, ob_ * 512:(ob_ + 1) * 512]))
                        slot["busy"] = [e_2]
                        fin.append(e_2)
                        e_st = dma("scalar", f"Do{d_ost.i % 2}",
                                   d_out[b2, pt * 128:(pt + 1) * 128, ob * 512:(ob + 1) * 512],
                                   ost["p"], deps=[e_2])
                        ost["busy"] = [e_st]
                    gx["busy"] = fin
                    xb2["busy"] = fin
                state["ev_wout_T"] = sch.ev("T")

            for b in range(B):
                xnbf = d_xnbf.get()
                e_ld = dma("sync", "DxN", d_xnbf3,
                           d_scr_xnbf_p[:, :, b * HL:(b + 1) * HL],
                           deps=xnbf["busy"] + ev_scr_bf)
                ev_xnbf = e_ld

                def win_qkv(col_base, M, n_out, dest):
                    for oc in range(n_out):
                        cc = col_base + oc * M
                        slot = rot_f.get()
                        slot4 = rot_f.get()
                        e1 = e2 = None
                        for kc in range(NCH):
                            dep0 = ([ev_xnbf, ev_dwq] + slot["busy"] + slot4["busy"]) \
                                if kc == 0 else []
                            e1 = mm(dep0, slot["p"][0:M, :], d_WQKV[kc][:, cc:cc + M],
                                    xnbf["p"][kc][:, 0:512], kc == 0, kc == NCH - 1)
                            e2 = mm([], slot4["p"][0:M, 0:4], d_WQKV[kc][:, cc:cc + M],
                                    xnbf["p"][kc][:, 512:516], kc == 0, kc == NCH - 1)
                        evs = dest(oc, slot["p"], slot4["p"], e1, e2)
                        slot["busy"] = list(evs)
                        slot4["busy"] = list(evs)

                def full_dest_w(dst, war_key):
                    def f(oc, ps, ps4, e1, e2):
                        war = [state.get(war_key)] if oc == 0 else []
                        ea = act([e1] + war, dst[oc][:, 0:512], ps[0:128, :], AF.Copy)
                        eb = act([e2], dst[oc][:, 512:516], ps4[0:128, 0:4], AF.Copy)
                        return [ea, eb]
                    return f

                win_qkv(0, 128, 8, full_dest_w(d_qTb, "d_prods_done"))
                win_qkv(1024, 128, 8, full_dest_w(d_kTb, "d_prods_done"))
                ev_qkvb = sch.ev("S")
                if b >= 1:
                    d_stage2(b - 1, state["ev_op_prev"])
                if b == 0:
                    tap("dq0", d_qTb[0], [ev_qkvb])
                    tap("dk0", d_kTb[0], [ev_qkvb])

                sc_evs = {}
                pssb = []
                for dl in range(-2, 3):
                    e_mm2 = None
                    for c in range(NCH):
                        pr = d_prod.get()
                        e_p = vec([ev_qkvb] + pr["busy"],
                                  lambda E, o=pr["p"], q=d_qTb[c], k=d_kTb[c], d_=dl:
                                  E.tensor_mul(o, q[:, 2:514], k[:, 2 + d_:514 + d_]))
                        e_mm2 = mm([e_p] + (pssb if c == 0 else []), pss, s_Ewin[c], pr["p"],
                                   c == 0, c == NCH - 1)
                        pr["busy"] = [e_mm2]
                    e_ev = act([e_mm2], d_scf[dl], pss, AF.Copy)
                    pssb = [e_ev]
                    sc_evs[dl] = e_ev
                for dl, qoff, koff in ((3, 2, 5), (-3, 3, 0)):
                    e_mm2 = None
                    for c in range(NCH):
                        pr = d_prod.get()
                        e_p = vec([ev_qkvb] + pr["busy"],
                                  lambda E, o=pr["p"], q=d_qTb[c], k=d_kTb[c], qo=qoff, ko=koff:
                                  E.tensor_tensor(
                                      o[:, 0:256],
                                      q.rearrange("p (n two) -> p n two", two=2)[:, qo // 2: qo // 2 + 256, qo % 2],
                                      k.rearrange("p (n two) -> p n two", two=2)[:, ko // 2: ko // 2 + 256, ko % 2],
                                      ALU.mult))
                        e_mm2 = mm([e_p] + (pssb if c == 0 else []), pss[:, 0:256], s_Ewin[c],
                                   pr["p"][:, 0:256], c == 0, c == NCH - 1)
                        pr["busy"] = [e_mm2]
                    e_ev = act([e_mm2], d_scp[dl], pss[:, 0:256], AF.Copy)
                    pssb = [e_ev]
                    sc_evs[dl] = e_ev
                if b == 0:
                    tap("sc0", d_scf[0], [sc_evs[0]])

                state["d_prods_done"] = sch.ev("V")
                win_qkv(2048, 128, 8, full_dest_w(d_vTb, "d_comb_done"))
                ev_qkv_v = sch.ev("S")
                xnbf["busy"] = [sch.ev("T")]

                def sview(dl, parity):
                    if dl in (3, -3):
                        return d_scp[dl][0:16, :]
                    return d_scf[dl].rearrange("p (n two) -> p n two", two=2)[0:16, :, parity]

                def cview(dl, parity):
                    return d_cdl[dl].rearrange("p (n two) -> p n two", two=2)[0:16, :, parity]

                groups = [
                    (0, [-2, -1, 0, 1], s_maskA, False),
                    (0, [0, 1, 2, 3], s_maskB, True),
                    (1, [-3, -2, -1, 0], s_maskA, False),
                    (1, [-1, 0, 1, 2], s_maskB, True),
                ]
                e_sm_deps = list(sc_evs.values())
                ev_cd = []
                for parity, dls, mask, second in groups:
                    svs = [sview(d2, parity) for d2 in dls]
                    e = vec(e_sm_deps + ev_cd, lambda E, a=svs[0], b2=svs[1]: E.tensor_max(d_sm1, a, b2))
                    for sv in svs[2:]:
                        e = vec([e], lambda E, a=sv: E.tensor_max(d_sm1, d_sm1, a))
                    e_exps = []
                    for i2, sv in enumerate(svs):
                        ex1 = vec([e], lambda E, a=sv, o=d_e[i2]: E.tensor_sub(o, a, d_sm1))
                        ex2 = act([ex1], d_e[i2], d_e[i2], AF.Exp)
                        e_exps.append(ex2)
                    e3 = vec(e_exps[0:2], lambda E: E.tensor_add(d_sm2, d_e[0], d_e[1]))
                    for i2 in (2, 3):
                        e3 = vec([e3, e_exps[i2]], lambda E, a=d_e[i2]: E.tensor_add(d_sm2, d_sm2, a))
                    e4 = vec([e3], lambda E: E.reciprocal(d_sm2, d_sm2))
                    e5 = vec([e4], lambda E, m=mask, p2=parity: E.tensor_tensor(
                        d_msk, d_sm2, m.rearrange("p (n two) -> p n two", two=2)[0:16, :, p2], ALU.mult))
                    for idx, d2 in enumerate(dls):
                        cv = cview(d2, parity)
                        if second and idx < 2:
                            e6 = vec([e5], lambda E, a=d_e[idx]: E.tensor_mul(d_sm1, a, d_msk))
                            e7 = vec([e6], lambda E, c2=cv: E.tensor_tensor(c2, c2, d_sm1, ALU.add))
                            ev_cd.append(e7)
                        else:
                            e6 = vec([e5], lambda E, a=d_e[idx], c2=cv: E.tensor_tensor(c2, a, d_msk, ALU.mult))
                            ev_cd.append(e6)
                ev_cd_all = sch.ev("V")
                if b == 0:
                    tap("cd0", d_cdl[0], [ev_cd_all])

                for c in range(NCH):
                    e_acc = None
                    for i2, dl in enumerate([-2, -1, 0, 1, 2, 3, -3]):
                        slot = rot_f.get()
                        tp2 = d_tmp2[i2 % 2]
                        tpev = state.get(f"dtp{i2 % 2}")
                        if dl in (3, -3):
                            parity = 0 if dl == 3 else 1
                            e_x = mm([ev_cd_all] + slot["busy"], slot["p"][:, 0:256], s_OH[c],
                                     cview(dl, parity), True, True)
                            hp0 = 2 + dl + parity
                            vsh = d_vTb[c].rearrange("p (n two) -> p n two", two=2)
                            vv = vsh[:, hp0 // 2: hp0 // 2 + 256, hp0 % 2]
                            ov = d_op[c].rearrange("p (n two) -> p n two", two=2)[:, :, parity]
                            e_m2 = vec([e_x, tpev, ev_qkv_v], lambda E, ps=slot["p"], v=vv, t2=tp2: E.tensor_tensor(
                                t2[:, 0:256], ps[:, 0:256], v, ALU.mult))
                            slot["busy"] = [e_m2]
                            e_acc = vec([e_m2, e_acc], lambda E, o=ov, t2=tp2:
                                        E.tensor_tensor(o, o, t2[:, 0:256], ALU.add))
                            state[f"dtp{i2 % 2}"] = e_acc
                        else:
                            e_x = mm([ev_cd_all] + slot["busy"], slot["p"], s_OH[c], d_cdl[dl], True, True)
                            vv = d_vTb[c][:, 2 + dl: 514 + dl]
                            if i2 == 0:
                                e_acc = vec([e_x, state.get("dop_cv"), ev_qkv_v],
                                            lambda E, ps=slot["p"], v=vv, o=d_op[c]: E.tensor_mul(o, ps, v))
                                slot["busy"] = [e_acc]
                            else:
                                e_m2 = vec([e_x, tpev], lambda E, ps=slot["p"], v=vv, t2=tp2:
                                           E.tensor_mul(t2, ps, v))
                                slot["busy"] = [e_m2]
                                e_acc = vec([e_m2, e_acc], lambda E, o=d_op[c], t2=tp2:
                                            E.tensor_add(o, o, t2))
                                state[f"dtp{i2 % 2}"] = e_acc
                    e_cv = act([e_acc, state.get("ev_wout_T")], d_opb[c], d_op[c], AF.Copy)
                    state["dop_cv"] = e_cv
                ev_op = sch.ev("S")
                state["d_comb_done"] = sch.ev("V")
                state["ev_op_prev"] = ev_op
                if b == 0:
                    tap("op0", d_op[0], [ev_op])

            d_stage2(B - 1, state["ev_op_prev"])

        for _nm in ("Do0", "Do1"):
            if sch.cnt.get(_nm, 0) > 0:
                sch.ops["sync"].append(
                    lambda E, s=sch.sem(_nm), v=sch.cnt[_nm]: E.wait_ge(s, v))

        with nc.Block() as block:
            sch.emit_all(block)

    return nc


_PROG_CACHE = {}


def _get_prog():
    if "p" not in _PROG_CACHE:
        _PROG_CACHE["p"] = build_program()
    return _PROG_CACHE["p"]


def make_in_maps(inputs):
    x = np.ascontiguousarray(np.asarray(inputs["L0_attn_x"], np.float32))
    base = dict(
        wqkv_a=np.ascontiguousarray(np.asarray(inputs["L0_attn_w_qkv"], np.float32)),
        wout_a=np.ascontiguousarray(np.asarray(inputs["L0_attn_w_out"], np.float32)),
        bout=np.ascontiguousarray(np.asarray(inputs["L0_attn_b_out"], np.float32).reshape(1, C)),
        ln1w=np.ascontiguousarray(np.asarray(inputs["L0_ffn_ln1_weight"], np.float32).reshape(1, C)),
        ln1b=np.ascontiguousarray(np.asarray(inputs["L0_ffn_ln1_bias"], np.float32).reshape(1, C)),
        ln2w=np.ascontiguousarray(np.asarray(inputs["L0_ffn_ln2_weight"], np.float32).reshape(1, C)),
        ln2b=np.ascontiguousarray(np.asarray(inputs["L0_ffn_ln2_bias"], np.float32).reshape(1, C)),
        wqkv_f=np.ascontiguousarray(np.asarray(inputs["L0_ffn_qkv_weight"], np.float32)),
        wproj=np.ascontiguousarray(np.asarray(inputs["L0_ffn_proj_weight"], np.float32)),
        wrout=np.ascontiguousarray(np.asarray(inputs["L0_ffn_router_weight"], np.float32)),
        w1=np.ascontiguousarray(np.asarray(inputs["L0_ffn_moe_w1"], np.float32)),
        w2=np.ascontiguousarray(np.asarray(inputs["L0_ffn_moe_w2"], np.float32)),
    )
    in_maps = []
    for core in range(NCORES):
        lo = core * PB
        xh = np.zeros((B, HL, C), np.float32)
        s0 = max(0, lo - 2)
        e0 = min(L, lo + PB + 2)
        xh[:, (s0 - (lo - 2)):(s0 - (lo - 2)) + (e0 - s0), :] = x[:, s0:e0, :]
        bc = np.full((1, PB), 2.0, np.float32)
        mA = np.ones((16, PB), np.float32)
        mB = np.ones((16, PB), np.float32)
        if core == 0:
            bc[0, 0:2] = 1.0
            mA[:, 0:2] = 0.0
        if core == NCORES - 1:
            bc[0, PB - 2:PB] = 1.0
            mB[:, PB - 2:PB] = 0.0
        m = dict(base)
        m.update(xh=np.ascontiguousarray(xh), bcount=bc, maskA=mA, maskB=mB)
        in_maps.append(m)
    return in_maps


def kernel(**inputs):
    nc = _get_prog()
    in_maps = make_in_maps(inputs)
    res = run_bass_kernel_spmd(nc, in_maps, core_ids=list(range(NCORES)))
    out = np.empty((B, L, C), np.float32)
    for core in range(NCORES):
        out[:, core * PB:(core + 1) * PB, :] = res.results[core]["out"]
    return out
